# version 13
# baseline (speedup 1.0000x reference)
import math
from contextlib import ExitStack
from types import SimpleNamespace

import numpy as np
import ml_dtypes

import concourse.bass as bass
import concourse.mybir as mybir
from concourse.bass_utils import run_bass_kernel_spmd

F32 = mybir.dt.float32
BF16 = mybir.dt.bfloat16
AF = mybir.ActivationFunctionType
ALU = mybir.AluOpType
AX = mybir.AxisListType
NEG = -1.0e30


class CFG:

    def __init__(self, **kw):
        self.B = 4
        self.S = 4096
        self.HK = 16
        self.E = 16
        self.F = 1024
        self.FS = 2048
        self.NC = 8
        self.TQ = 512
        self.FFN_TP = 512
        self.__dict__.update(kw)
        assert self.S % 128 == 0 and self.HK % self.NC == 0
        self.D = self.HK * 128
        self.T = self.B * self.S
        self.Tc = self.T // self.NC
        self.HPC = self.HK // self.NC
        self.KC = self.D // 128
        self.NCH = self.S // 128
        assert self.Tc % 128 == 0
        assert self.TQ % 128 == 0 and self.S % self.TQ == 0
        assert self.FFN_TP % 128 == 0 and self.Tc % self.FFN_TP == 0
        assert self.F % 128 == 0 and self.FS % 512 == 0


class Ch:

    def __init__(self, nc, name):
        self.sem = nc.alloc_semaphore(name=name)
        self.n = 0
        self.name = name


class Rec:

    ENG = ("t", "v", "a", "g", "s")

    def __init__(self, nc):
        self.nc = nc
        self.limit = None
        self.count = 0
        self.items = {e: [] for e in self.ENG}
        self.last_op = {e: None for e in self.ENG}
        self.dma_out = {e: {} for e in self.ENG}
        self.pending_bar = {e: None for e in self.ENG}
        self.waited = {e: {} for e in self.ENG}

    def _arm(self, eng):
        if self.pending_bar[eng] is not None:
            ch, val = self.pending_bar[eng]
            self.pending_bar[eng] = None
            self._wait(eng, ch, val)

    def _wait(self, eng, ch, val):
        if self.waited[eng].get(ch, -1) >= val:
            return
        self.waited[eng][ch] = val
        self.items[eng].append(("wait", ch, val))

    def op(self, eng, fn, tick=None, dma=None):
        self.count += 1
        if self.limit is not None and self.count > self.limit:
            return ["op", lambda e: None, []]
        self._arm(eng)
        ticks = []
        if tick is not None:
            chx, k = tick if isinstance(tick, tuple) else (tick, 1)
            ticks.append((chx, k))
            chx.n += k
        if dma is not None:
            ticks.append((dma, 16))
            dma.n += 16
            self.dma_out[eng][dma] = dma.n
        assert len(ticks) <= 1, "max 1 sem update per instruction"
        item = ["op", fn, ticks]
        self.items[eng].append(item)
        self.last_op[eng] = item
        return item

    def wait(self, eng, ch, val=None):
        if self.limit is not None and self.count > self.limit:
            return
        self._arm(eng)
        self._wait(eng, ch, ch.n if val is None else val)

    def bar(self, bar_ch):
        for e in self.ENG:
            if self.last_op[e] is None:
                continue
            outs = self.dma_out[e]
            if outs:
                self._arm(e)
                for chx, val in list(outs.items()):
                    self._wait(e, chx, val)
                outs.clear()
                self.op(e, lambda en: en.nop(), tick=bar_ch)
            elif len(self.last_op[e][2]) < 1:
                self.last_op[e][2].append((bar_ch, 1))
                bar_ch.n += 1
            else:
                self.op(e, lambda en: en.nop(), tick=bar_ch)
            self.last_op[e] = None
        tgt = bar_ch.n
        for e in self.ENG:
            self.pending_bar[e] = (bar_ch, tgt)

    def replay(self, block):
        def runner(items):
            def go(eng):
                for it in items:
                    if it[0] == "op":
                        inst = it[1](eng)
                        for chx, k in it[2]:
                            inst.then_inc(chx.sem, k)
                    else:
                        eng.wait_ge(it[1].sem, it[2])
            return go

        block.tensor(runner(self.items["t"]))
        block.vector(runner(self.items["v"]))
        block.scalar(runner(self.items["a"]))
        block.gpsimd(runner(self.items["g"]))
        block.sync(runner(self.items["s"]))




def build_nc(cfg, debug=(), upto=99, rec_limit=None, skip=()):
    c = cfg
    nc = bass.Bass("TRN2", target_bir_lowering=False)
    G = SimpleNamespace(c=c, nc=nc, debug=set(debug), dbg_out={}, upto=upto,
                        skip=set(skip), in_names=[])
    G._rec_limit = rec_limit

    def ein(name, shape, dtype=F32):
        ap = nc.dram_tensor(name, list(shape), dtype, kind="ExternalInput").ap()
        setattr(G, name, ap)
        G.in_names.append(name)
        return ap

    def idram(name, shape, dtype):
        ap = nc.dram_tensor(name, list(shape), dtype, kind="Internal").ap()
        setattr(G, name, ap)
        return ap

    def idram2(name, shape, dtype, space):
        ap = nc.dram_tensor(name, list(shape), dtype, kind="Internal",
                            addr_space=space).ap()
        setattr(G, name, ap)
        return ap

    ein("x0", (c.Tc, c.D))
    if "in_w" not in G.skip:
        ein("wqkvz", (c.D, c.HPC * 512))
        ein("wba", (c.D, c.HPC * 2))
        ein("convw", (3 * c.HPC * 128, 4))
        ein("dtb_alog", (2, c.HPC))
        ein("normw", (1, 128))
        ein("wout", (c.D, c.D))
        ein("wq", (c.HPC * 128, c.D))
        ein("wk", (c.HPC * 128, c.D))
        ein("wv", (c.HPC * 128, c.D))
        ein("bqkv", (3, c.HPC * 128))
        ein("wo", (c.D, c.D))
        ein("bo", (1, c.D))
        ein("routwT", (c.D, c.E))
        ein("w1", (c.E, c.D, c.F))
        ein("v1", (c.E, c.D, c.F))
        ein("w2", (c.E, c.F, c.D))
        ein("sg", (c.FS, c.D))
        ein("su", (c.FS, c.D))
        ein("sd", (c.D, c.FS))
    if "in_c" not in G.skip:
        if "c1" not in G.skip:
            ein("identf", (128, 128))
        if "c2" not in G.skip:
            ein("triu1", (128, 128))
        if "c3" not in G.skip:
            ein("e127", (128, 128))
        if "c4" not in G.skip:
            ein("cosb", (128, c.S))
        if "c5" not in G.skip:
            ein("sinb", (128, c.S))
        if "c6" not in G.skip:
            ein("ropeperm", (128, 128))

    G.out = nc.dram_tensor("out", [c.Tc, c.D], F32, kind="ExternalOutput").ap()
    if "gates" in G.debug:
        G.dbg_gates = nc.dram_tensor("dbg_gates", [c.Tc, c.E], F32,
                                     kind="ExternalOutput").ap()
    if "scores" in G.debug:
        G.dbg_scores = nc.dram_tensor("dbg_scores", [c.Tc, c.E], F32,
                                      kind="ExternalOutput").ap()

    if "int_ag" not in G.skip:
        idram("ag_h1_in", (c.Tc, c.D), BF16)
        idram2("ag_h1", (c.T, c.D), BF16, "Shared")
        idram("agin_na", (c.T, c.HPC * 128), BF16)
        idram2("ag_na", (c.NC * c.T, c.HPC * 128), BF16, "Shared")
        idram("my_na", (c.Tc, c.D), BF16)
        idram("ag_h3_in", (c.Tc, c.D), BF16)
        idram2("ag_h3", (c.T, c.D), BF16, "Shared")
        idram("agin_nb", (c.T, c.HPC * 128), BF16)
        idram2("ag_nb", (c.NC * c.T, c.HPC * 128), BF16, "Shared")
        idram("my_nb", (c.Tc, c.D), BF16)
    if "int_x" not in G.skip:
        idram("x_cur", (c.Tc, c.D), F32)
        idram("x_cur2", (c.Tc, c.D), F32)
        idram("h_loc", (c.Tc, c.D), BF16)
        idram("rowscr", (c.B * c.HPC * 2 * (c.S // 128), 128), F32)
    if "int_w" not in G.skip:
        idram("wq_b", (c.HPC * 128, c.D), BF16)
        idram("wk_b", (c.HPC * 128, c.D), BF16)
        idram("wv_b", (c.HPC * 128, c.D), BF16)
        idram("wo_b", (c.D, c.D), BF16)
        idram("w1_b", (c.E, c.D, c.F), BF16)
        idram("v1_b", (c.E, c.D, c.F), BF16)
        idram("w2_b", (c.E, c.F, c.D), BF16)
        idram("sg_b", (c.FS, c.D), BF16)
        idram("su_b", (c.FS, c.D), BF16)
        idram("sd_b", (c.D, c.FS), BF16)
        idram("routwT_b", (c.D, c.E), BF16)

    rec = G.rec = Rec(nc)
    rec.limit = G._rec_limit
    chans = {}

    def ch(name):
        if name not in chans:
            chans[name] = Ch(nc, name)
        return chans[name]

    G.ch = ch

    def dbg(tag, src_ap, after_ch=None):
        if tag not in G.debug:
            return
        o = nc.dram_tensor("dbg_" + tag, list(src_ap.shape),
                           src_ap.dtype, kind="ExternalOutput").ap()
        G.dbg_out[tag] = o
        cd = ch("c_dbg")
        if after_ch is not None:
            rec.wait("s", after_ch)
        rec.op("s", lambda e: e.dma_start(out=o, in_=src_ap), dma=cd)
        rec.wait("s", cd)

    G.dbg = dbg

    with ExitStack() as ST:
        G.ST = ST

        def sb(name, shape, dtype):
            h = ST.enter_context(nc.sbuf_tensor(name, list(shape), dtype))
            setattr(G, "s_" + name, h)
            return h

        G.sb = sb
        if "sbufp" in G.skip:
            G.upto = -9
        _csts = {}
        _cst_arena = ST.enter_context(nc.sbuf_tensor("cst_arena", [128, 64], F32))

        def cst(val):
            if val not in _csts:
                i = len(_csts)
                assert i < 64
                rec.op("v", lambda e, i=i, v=val: e.memset(
                    _cst_arena[:, i:i + 1], v), tick=ch("c_cst"))
                _csts[val] = i
            for e in Rec.ENG:
                if e != "v":
                    rec.wait(e, ch("c_cst"))
            return _cst_arena[:, _csts[val]:_csts[val] + 1]

        G.cst = cst
        if "sbufp" not in G.skip:
            sb("identb", (128, 128), BF16)
            sb("identf_s", (128, 128), F32)
            sb("ones_col_b", (128, 1), BF16)
            sb("ones_row_f", (1, 512), F32)
            sb("ones_row_b", (1, 512), BF16)
            sb("one11_f", (1, 1), F32)
            sb("one11_b", (1, 1), BF16)
            sb("wnorm_bc", (128, 128), F32)
            sb("mask_ui", (128, 128), F32)
            sb("mask_us", (128, 128), F32)
            sb("triu1_s", (128, 128), F32)
            sb("e127_s", (128, 128), F32)

        if G.upto <= -9:
            tmin = G.sb("tmin", (128, c.D), F32)
            cmin = ch("cmin")
            rec.op("s", lambda e: e.dma_start(out=tmin[:, :],
                                              in_=G.x0[0:128, :]), dma=cmin)
            rec.wait("s", cmin)
            rec.op("s", lambda e: e.dma_start(out=G.out[0:128, :],
                                              in_=tmin[:, :]), dma=cmin)
            rec.wait("s", cmin)
            with nc.Block() as block:
                rec.replay(block)
            return G
        p_prologue(G)
        if G.upto >= -1:
            p_ln(G, src=G.x0, dst=G.ag_h1_in, tag="ln1", resid_src=None)
        if G.upto >= 0:
            rec.op("g", lambda e: e.collective_compute(
                "AllGather", ALU.bypass, replica_groups=[list(range(c.NC))],
                ins=[G.ag_h1_in[:, :]], outs=[G.ag_h1[:, :]]), tick=ch("cc_h1"))
            dbg("h1", G.ag_h1, after_ch=ch("cc_h1"))

        rec.bar(ch("bar"))
        if G.upto >= 1:
            p_rest(G)

        rec.bar(ch("bar"))
        rec.wait("s", ch("bar"))

        with nc.Block() as block:
            rec.replay(block)

    return G


def p_prologue(G):
    c, nc, rec, ch = G.c, G.nc, G.rec, G.ch
    cl = ch("c_const")

    def dma(eng, out_ap, in_ap, chan=None):
        rec.op(eng, lambda e, o=out_ap, i=in_ap: e.dma_start(out=o, in_=i),
               dma=chan or cl)

    dma("s", G.s_identf_s[:, :], G.identf[:, :])
    dma("s", G.s_triu1_s[:, :], G.triu1[:, :])
    dma("s", G.s_e127_s[:, :], G.e127[:, :])
    dma("g", G.s_identb[:, :], G.identf[:, :], chan=ch("c_const_g"))
    rec.op("v", lambda e: e.memset(G.s_ones_col_b[:, :], 1.0))
    rec.op("v", lambda e: e.memset(G.s_ones_row_f[:, :], 1.0))
    rec.op("v", lambda e: e.memset(G.s_ones_row_b[:, :], 1.0))
    rec.op("v", lambda e: e.memset(G.s_one11_f[:, :], 1.0))
    rec.op("v", lambda e: e.memset(G.s_one11_b[:, :], 1.0), tick=ch("c_v"))
    gs = ch("c_g")
    rec.op("g", lambda e: e.memset(G.s_mask_ui[:, :], 0.0), tick=gs)
    rec.op("g", lambda e: e.memset(G.s_mask_us[:, :], 0.0), tick=gs)
    rec.wait("g", gs)
    rec.op("g", lambda e: e.affine_select(
        G.s_mask_ui[:, :], G.s_mask_ui[:, :], [[1, 128]], ALU.is_ge, NEG,
        base=0, channel_multiplier=-1), tick=gs)
    rec.op("g", lambda e: e.affine_select(
        G.s_mask_us[:, :], G.s_mask_us[:, :], [[1, 128]], ALU.is_gt, NEG,
        base=0, channel_multiplier=-1), tick=gs)
    nw = G.sb("nw_row", (1, 128), F32)
    dma("s", nw[:, :], G.normw[:, :])
    with nc.psum_tensor("ps_c0", [128, 128], F32) as ps:
        rec.wait("t", cl)
        rec.wait("t", ch("c_const_g"))
        rec.wait("t", ch("c_v"))
        rec.op("t", lambda e: e.matmul(ps[:, :], G.s_ones_row_f[:1, :128],
                                       nw[:, :], start=True, stop=True),
               tick=ch("c_t"))
        rec.wait("v", ch("c_t"))
        rec.op("v", lambda e: e.tensor_copy(G.s_wnorm_bc[:, :], ps[:, :]),
               tick=ch("c_v"))


def p_ln(G, src, dst, tag, resid_src=None, resid_ps=None, x_dst=None):
    c, nc, rec, ch = G.c, G.nc, G.rec, G.ch
    nt = c.Tc // 128
    with ExitStack() as S:
        xa = S.enter_context(nc.sbuf_tensor(tag + "_xa", [128, c.D], F32))
        xb = S.enter_context(nc.sbuf_tensor(tag + "_xb", [128, c.D], F32))
        sq = S.enter_context(nc.sbuf_tensor(tag + "_sq", [128, c.D], F32))
        hb = S.enter_context(nc.sbuf_tensor(tag + "_h", [128, c.D], BF16))
        cols = S.enter_context(nc.sbuf_tensor(tag + "_c", [128, 8], F32))
        cld, cv, ca, cst = (ch(tag + k) for k in ("_ld", "_v", "_a", "_st"))
        vdone = [0, 0]
        adone = [0, 0]
        for i in range(nt):
            sl = i % 2
            xt = (xa, xb)[sl]
            co = 4 * sl
            rec.wait("s", cv, vdone[sl])
            rec.wait("s", ca, adone[sl])
            rec.op("s", lambda e, o=xt, t0=i * 128: e.dma_start(
                out=o[:, :], in_=src[t0:t0 + 128, :]), dma=cld)
            rec.wait("v", cld)
            rec.wait("v", ca, adone[sl])
            rec.op("v", lambda e, x=xt, co=co: e.reduce_sum(
                cols[:, co:co + 1], x[:, :], AX.X), tick=cv)
            rec.wait("a", cv)
            rec.op("a", lambda e, co=co: e.activation(
                cols[:, co + 1:co + 2], cols[:, co:co + 1], AF.Copy,
                scale=1.0 / c.D), tick=ca)
            rec.wait("v", ca)
            rec.op("v", lambda e, x=xt, co=co: e.tensor_scalar_sub(
                x[:, :], x[:, :], cols[:, co + 1:co + 2]), tick=cv)
            rec.wait("a", cv)
            rec.op("a", lambda e, x=xt: e.activation(
                sq[:, :], x[:, :], AF.Square), tick=ca)
            rec.wait("v", ca)
            rec.op("v", lambda e, co=co: e.reduce_sum(
                cols[:, co + 2:co + 3], sq[:, :], AX.X), tick=cv)
            eps_ap = G.cst(1e-5)
            rec.wait("a", cv)
            rec.op("a", lambda e, co=co, b=eps_ap: e.activation(
                cols[:, co + 3:co + 4], cols[:, co + 2:co + 3], AF.Sqrt,
                scale=1.0 / c.D, bias=b), tick=ca)
            adone[sl] = ca.n
            rec.wait("v", ca)
            rec.op("v", lambda e, co=co: e.reciprocal(
                cols[:, co + 2:co + 3], cols[:, co + 3:co + 4]), tick=cv)
            rec.wait("v", cv)
            rec.wait("v", cst)
            rec.op("v", lambda e, x=xt, co=co: e.tensor_scalar_mul(
                hb[:, :], x[:, :], cols[:, co + 2:co + 3]), tick=cv)
            vdone[sl] = cv.n
            rec.wait("s", cv)
            rec.op("s", lambda e, t0=i * 128: e.dma_start(
                out=dst[t0:t0 + 128, :], in_=hb[:, :]), dma=cst)
        rec.wait("s", cst)
        rec.wait("g", cst)


def p_rest(G):
    c, ch = G.c, G.ch
    rec = G.rec
    if G.upto < 1:
        return
    p_attn_a(G)
    rec.bar(ch("bar"))
    if G.upto < 2:
        return
    p_mixer_out(G, tag="mo1", ag=G.ag_na, my=G.my_na, cc="cc_na",
                w_direct=G.wout, bias=None, x_src=G.x0, x_dst=G.x_cur,
                h_dst=G.h_loc, dep=None)
    G.dbg("x1", G.x_cur, after_ch=ch("mo1_st"))
    G.dbg("h2", G.h_loc, after_ch=ch("mo1_st"))
    rec.bar(ch("bar"))
    if G.upto < 3:
        return
    p_ffn(G, tag="f1", h_src=G.h_loc, x_src=G.x_cur, x_dst=G.x_cur2,
          h_dst=G.ag_h3_in, last=False, dep=ch("mo1_st"))
    G.dbg("x2", G.x_cur2, after_ch=ch("f1_st"))
    G.rec.op("g", lambda e: e.collective_compute(
        "AllGather", ALU.bypass,
        replica_groups=[list(range(c.NC))],
        ins=[G.ag_h3_in[:, :]], outs=[G.ag_h3[:, :]]), tick=ch("cc_h3"))
    G.dbg("h3", G.ag_h3, after_ch=ch("cc_h3"))
    rec.bar(ch("bar"))
    if G.upto < 4:
        return
    p_attn_b(G)
    rec.bar(ch("bar"))
    if G.upto < 5:
        return
    p_mixer_out(G, tag="mo2", ag=G.ag_nb, my=G.my_nb, cc="cc_nb",
                w_direct=None, bias=G.bo, x_src=G.x_cur2, x_dst=G.x_cur,
                h_dst=G.h_loc, dep=ch("f1_st"))
    rec.bar(ch("bar"))
    if G.upto < 6:
        return
    p_ffn(G, tag="f2", h_src=G.h_loc, x_src=G.x_cur, x_dst=G.out,
          h_dst=None, last=True, dep=ch("mo2_st"))


def p_attn_a(G):
    c, nc, rec, ch = G.c, G.nc, G.rec, G.ch
    HPC, TQ, KC = c.HPC, c.TQ, c.KC
    NT = c.S // TQ
    NSUB = TQ // 128
    FQ = 3 * HPC
    ZW = HPC * 128 + 2 * HPC
    MS = 13
    NLV = 7
    with ExitStack() as S:
        def sb(name, shape, dtype):
            return S.enter_context(nc.sbuf_tensor("aa_" + name, list(shape), dtype))

        hT = [sb("hT0", (128, KC * TQ), BF16), sb("hT1", (128, KC * TQ), BF16)]
        wqkv = sb("wqkv", (128, KC * FQ * 128), BF16)
        wzba = sb("wzba", (128, KC * ZW), BF16)
        convd = sb("convd", (128, FQ * 4 * 128), BF16)
        convc = sb("convc", (128, FQ * 4), F32)
        qkvraw = sb("qkvraw", (128, FQ * (TQ + 3)), BF16)
        qkvs = [sb("qkvs0", (128, FQ * TQ), BF16), sb("qkvs1", (128, FQ * TQ), BF16)]
        ktm = [sb("ktm0", (128, HPC * TQ), BF16), sb("ktm1", (128, HPC * TQ), BF16)]
        vtm = [sb("vtm0", (128, HPC * TQ), BF16), sb("vtm1", (128, HPC * TQ), BF16)]
        zsil = [sb("zs0", (128, NSUB * HPC * 128), BF16),
                sb("zs1", (128, NSUB * HPC * 128), BF16)]
        mats = sb("mats", (128, HPC * MS * NSUB), F32)
        rowb = sb("rowb", (1, HPC * 3 * TQ), F32)
        rowt = sb("rowt", (1, 2 * 512), F32)
        rowft = sb("rowft", (1, HPC * TQ), F32)
        rowbt = sb("rowbt", (1, HPC * TQ), F32)
        rowbb = sb("rowbb", (1, HPC * 3 * TQ), BF16)
        rowsq = sb("rowsq", (128, TQ), BF16)
        rkcol = sb("rkcol", (128, HPC * (TQ // 128)), F32)
        rawAU = sb("rawAU", (128, 4 * 256), F32)
        rbt2 = [sb("rbtA", (16, 128), F32), sb("rbtB", (16, 128), F32)]
        headc = sb("headc", (128, 2 * HPC), F32)
        dtb_r = sb("dtb_r", (1, 2 * HPC), F32)
        Mb = sb("Mb", (128, 4 * 128), F32)
        Eb = sb("Eb", (128, 4 * 2 * 128), F32)
        mUb = sb("mUb", (128, 4 * 128), F32)
        mab = sb("mab", (128, 4 * 128), F32)
        Upow = sb("Upow", (128, 4 * NLV * 128), BF16)
        Tpow = sb("Tpow", (128, 4 * 2 * 128), BF16)
        Xb = sb("Xb", (128, 4 * 2 * 256), BF16)
        NSL = 2
        attT = sb("attT", (128, NSL * HPC * 128), BF16)
        wnT = sb("wnT", (128, NSL * HPC * 128), BF16)
        kdec = sb("kdec", (128, NSL * HPC * 128), BF16)
        qgT = sb("qgT", (128, NSL * HPC * 128), BF16)
        Xfin = sb("Xfin", (128, NSL * HPC * 256), BF16)
        vnew = sb("vnew", (128, HPC * 128), BF16)
        colsc = sb("colsc", (128, 4 * 8), F32)
        oc2 = sb("oc2", (128, HPC * 4), F32)
        Sf = sb("Sf", (128, HPC * 128), F32)
        Sb = sb("Sb", (128, HPC * 128), BF16)
        onorm = sb("onorm", (128, HPC * 128), BF16)
        ot1 = sb("ot1", (128, HPC * 128), F32)
        osq = sb("osq", (128, HPC * 128), F32)

        pb = [S.enter_context(nc.psum_tensor(f"aa_pb{i}", [128, 512], F32))
              for i in range(8)]
        DB = [pb[5], pb[1], pb[6], pb[2]]
        DBtr = [d[:, 320:384].bitcast(BF16) for d in DB]

        cw = ch("ca_w")
        cld2 = [ch("ca_ld0"), ch("ca_ld1")]
        ct = ch("ca_t")
        cv = ch("ca_v")
        ca = ch("ca_a")
        cg = ch("ca_g")
        cso = ch("ca_so")
        crw = ch("ca_rw")

        def T(fn):
            rec.op("t", fn, tick=ct)

        def V(fn):
            rec.op("v", fn, tick=cv)

        def A(fn):
            rec.op("a", fn, tick=ca)

        def P(fn):
            rec.op("g", fn, tick=cg)

        W = rec.wait
        res = {}

        def use(eng, *names):
            for n in names:
                for chx, val in res.get(n, ()):
                    rec.wait(eng, chx, val)

        def setr(name, *chans):
            res[name] = [(chx, chx.n) for chx in chans]

        for hh in range(HPC):
            rec.op("g", lambda e, hh=hh: e.dma_start(
                out=wqkv[:, :].rearrange("p (ch f) -> p ch f", f=FQ * 128)
                    [:, :, hh * 384:(hh + 1) * 384],
                in_=G.wqkvz.rearrange("(ch p) f -> p ch f", p=128)
                    [:, :, 512 * hh:512 * hh + 384]), dma=cw)
            rec.op("g", lambda e, hh=hh: e.dma_start(
                out=wzba[:, :].rearrange("p (ch f) -> p ch f", f=ZW)
                    [:, :, hh * 128:(hh + 1) * 128],
                in_=G.wqkvz.rearrange("(ch p) f -> p ch f", p=128)
                    [:, :, 512 * hh + 384:512 * hh + 512]), dma=cw)
        rec.op("g", lambda e: e.dma_start(
            out=wzba[:, :].rearrange("p (ch f) -> p ch f", f=ZW)
                [:, :, HPC * 128:],
            in_=G.wba.rearrange("(ch p) f -> p ch f", p=128)), dma=cw)
        rec.op("g", lambda e: e.dma_start(
            out=convc[:, :].rearrange("p (fc j) -> p fc j", j=4),
            in_=G.convw.rearrange("(fc p) j -> p fc j", p=128)), dma=cw)
        rec.wait("g", cw)
        for fc in range(FQ):
            for j in range(4):
                off = (fc * 4 + j) * 128
                P(lambda e, o=off: e.memset(convd[:, o:o + 128], 0.0))
                W("g", cg)
                P(lambda e, o=off, cc=fc * 4 + j: e.tensor_scalar_add(
                    convd[:, o:o + 128], convd[:, o:o + 128],
                    convc[:, cc:cc + 1]))
                W("g", cg)
                P(lambda e, o=off: e.affine_select(
                    convd[:, o:o + 128], convd[:, o:o + 128],
                    [[1, 128]], ALU.is_equal, 0.0, base=0,
                    channel_multiplier=-1))
        csld = ch("ca_sld")
        rec.op("s", lambda e: e.dma_start(
            out=dtb_r[:, 0:HPC], in_=G.dtb_alog[0:1, :]), dma=csld)
        rec.op("s", lambda e: e.dma_start(
            out=dtb_r[:, HPC:2 * HPC], in_=G.dtb_alog[1:2, :]), dma=csld)
        W("a", csld)
        A(lambda e: e.activation(dtb_r[:, HPC:2 * HPC], dtb_r[:, HPC:2 * HPC],
                                 AF.Exp))
        W("a", ca)
        A(lambda e: e.activation(dtb_r[:, HPC:2 * HPC], dtb_r[:, HPC:2 * HPC],
                                 AF.Copy, scale=-1.0))
        W("t", ca)
        T(lambda e: e.matmul(pb[3][:, 0:2 * HPC], G.s_ones_row_f[:1, :128],
                             dtb_r[:, :], start=True, stop=True))
        W("v", ct)
        V(lambda e: e.tensor_copy(headc[:, :], pb[3][:, 0:2 * HPC]))
        setr("pb3", cv)

        cwp = ch("c_wprep")
        rec.wait("g", ch("cc_h1"))
        for s_, d_ in ((G.routwT, G.routwT_b), (G.sg, G.sg_b),
                       (G.su, G.su_b), (G.sd, G.sd_b)):
            rec.op("g", lambda e, o=d_, i_=s_: e.dma_start(
                out=o[:, :], in_=i_[:, :]), dma=cwp)
        for s_, d_ in ((G.w1, G.w1_b), (G.v1, G.v1_b), (G.w2, G.w2_b)):
            for e_i in range(c.E):
                rec.op("g", lambda e, o=d_[e_i], i_=s_[e_i]: e.dma_start(
                    out=o, in_=i_), dma=cwp)
        for s_, d_ in ((G.wq, G.wq_b), (G.wk, G.wk_b), (G.wv, G.wv_b),
                       (G.wo, G.wo_b)):
            rec.op("g", lambda e, o=d_, i_=s_: e.dma_start(
                out=o[:, :], in_=i_[:, :]), dma=cwp)

        W("s", ch("cc_h1"))

        def load_hT(b, i, par):
            for chk in range(KC):
                rec.op("s", lambda e, b=b, i=i, par=par, chk=chk: e.dma_start(
                    out=hT[par][:, chk * TQ:(chk + 1) * TQ],
                    in_=G.ag_h1[b * c.S + i * TQ: b * c.S + (i + 1) * TQ,
                                chk * 128:(chk + 1) * 128],
                    transpose=True), dma=cld2[par])

        def stage_a_gen(b, i, par, first=False):
            W("t", cld2[par])
            W("t", cw)
            use("t", f"tdone{par}")
            W("a", cg)
            if first:
                W("g", ca)
                for fc in range(FQ):
                    P(lambda e, fc=fc: e.memset(
                        qkvraw[:, fc * (TQ + 3):fc * (TQ + 3) + 3], 0.0))
                halo_g = cg.n
            names = ("pb3", "pb4")
            bks = (pb[3], pb[4])
            kk = 0
            for fc in range(FQ):
                nm = names[kk % 2]
                ps = bks[kk % 2]
                kk += 1
                use("t", nm)
                for chk in range(KC):
                    T(lambda e, fc=fc, chk=chk, ps=ps, par=par: e.matmul(
                        ps[:, 0:512],
                        wqkv[:, chk * FQ * 128 + fc * 128:
                             chk * FQ * 128 + (fc + 1) * 128],
                        hT[par][:, chk * TQ: chk * TQ + 512],
                        start=(chk == 0), stop=(chk == KC - 1)))
                mmn = ct.n
                rec.wait("a", ct, mmn)
                A(lambda e, fc=fc, ps=ps: e.activation(
                    qkvraw[:, fc * (TQ + 3) + 3: fc * (TQ + 3) + 3 + 512],
                    ps[:, 0:512], AF.Copy))
                setr(nm, ca)
                yield
            if first:
                rec.wait("t", cg, halo_g)
            for fc in range(FQ):
                nm = names[kk % 2]
                ps = bks[kk % 2]
                kk += 1
                use("t", nm)
                for j in range(4):
                    T(lambda e, fc=fc, j=j, ps=ps: e.matmul(
                        ps[:, 0:512],
                        convd[:, (fc * 4 + j) * 128:(fc * 4 + j + 1) * 128],
                        qkvraw[:, fc * (TQ + 3) + j: fc * (TQ + 3) + j + 512],
                        start=(j == 0), stop=(j == 3)))
                mmn = ct.n
                rec.wait("a", ct, mmn)
                A(lambda e, fc=fc, ps=ps, par=par: e.activation(
                    qkvs[par][:, fc * TQ: fc * TQ + 512],
                    ps[:, 0:512], AF.Silu))
                setr(nm, ca)
                yield
            W("g", ca)
            for fc in range(FQ):
                P(lambda e, fc=fc: e.tensor_copy(
                    qkvraw[:, fc * (TQ + 3):fc * (TQ + 3) + 3],
                    qkvraw[:, fc * (TQ + 3) + TQ:fc * (TQ + 3) + TQ + 3]))
            for ts in range(NSUB):
                nm = names[kk % 2]
                psz = bks[kk % 2]
                kk += 1
                use("t", nm)
                for chk in range(KC):
                    T(lambda e, ts=ts, chk=chk, psz=psz, par=par: e.matmul(
                        psz[:, 0:ZW],
                        hT[par][:, chk * TQ + ts * 128:
                                chk * TQ + (ts + 1) * 128],
                        wzba[:, chk * ZW:(chk + 1) * ZW],
                        start=(chk == 0), stop=(chk == KC - 1)))
                mmn = ct.n
                rec.wait("a", ct, mmn)
                A(lambda e, ts=ts, psz=psz, par=par: e.activation(
                    zsil[par][:, ts * HPC * 128:(ts + 1) * HPC * 128],
                    psz[:, 0:HPC * 128], AF.Silu))
                for hh in range(HPC):
                    for s_i, slot in ((0, 0), (1, 1)):
                        A(lambda e, ts=ts, hh=hh, s_i=s_i, slot=slot, psz=psz:
                          e.activation(
                            mats[:, (hh * MS + slot) * NSUB + ts:
                                 (hh * MS + slot) * NSUB + ts + 1],
                            psz[:, HPC * 128 + 2 * hh + s_i:
                                HPC * 128 + 2 * hh + s_i + 1], AF.Copy))
                setr(nm, ca)
                yield
            setr(f"hTfree{par}", ct)
            for hh in range(HPC):
                for ts in range(NSUB):
                    for part, dstb in ((1, ktm), (2, vtm)):
                        nm = names[kk % 2]
                        ps = bks[kk % 2]
                        kk += 1
                        tr = ps[:, 0:64].bitcast(BF16)
                        use("t", nm)
                        T(lambda e, hh=hh, ts=ts, part=part, par=par,
                          tr=tr: e.transpose(
                            tr[:, 0:128],
                            qkvs[par][:, (3 * hh + part) * TQ + ts * 128:
                                      (3 * hh + part) * TQ + (ts + 1) * 128],
                            G.s_identb[:, :]))
                        mmn = ct.n
                        rec.wait("a", ct, mmn)
                        A(lambda e, hh=hh, ts=ts, dstb=dstb, par=par,
                          tr=tr: e.activation(
                            dstb[par][:, hh * TQ + ts * 128:
                                      hh * TQ + (ts + 1) * 128],
                            tr[:, 0:128], AF.Copy))
                        setr(nm, ca)
                        if kk % 2 == 0:
                            yield

        def mats_prep(b, i, par):
            colf = lambda hh, s: mats[:, (hh * MS + s) * NSUB:
                                      (hh * MS + s + 1) * NSUB]
            W("a", ca)
            W("a", cv)
            W("a", cg)
            for hh in range(HPC):
                A(lambda e, hh=hh: e.activation(colf(hh, 2), colf(hh, 0),
                                                AF.Sigmoid))
            W("a", ca)
            for hh in range(HPC):
                A(lambda e, hh=hh: e.activation(colf(hh, 3), colf(hh, 2),
                                                AF.Copy, scale=-1.0))
                A(lambda e, hh=hh: e.activation(colf(hh, 11), colf(hh, 1),
                                                AF.Abs,
                                                bias=headc[:, hh:hh + 1]))
            W("a", ca)
            for hh in range(HPC):
                A(lambda e, hh=hh: e.activation(colf(hh, 11), colf(hh, 11),
                                                AF.Exp, scale=-1.0))
            W("a", ca)
            b_one = G.cst(1.0)
            for hh in range(HPC):
                A(lambda e, hh=hh, bo=b_one: e.activation(
                    colf(hh, 11), colf(hh, 11), AF.Ln, bias=bo))
            W("v", ca)
            for hh in range(HPC):
                V(lambda e, hh=hh: e.tensor_scalar(
                    colf(hh, 12), colf(hh, 1), headc[:, hh:hh + 1], 0.0,
                    ALU.add, ALU.max))
            W("v", cv)
            for hh in range(HPC):
                V(lambda e, hh=hh: e.tensor_add(colf(hh, 12), colf(hh, 12),
                                                colf(hh, 11)))
            W("v", cv)
            for hh in range(HPC):
                V(lambda e, hh=hh: e.tensor_scalar_mul(
                    colf(hh, 10), colf(hh, 12),
                    headc[:, HPC + hh:HPC + hh + 1]))
            use("t", "pb3", "pb4")
            W("t", cv)
            for hh in range(HPC):
                T(lambda e, hh=hh: e.matmul(
                    pb[3][:, hh * NSUB:(hh + 1) * NSUB], G.s_triu1_s[:, :],
                    colf(hh, 10), start=True, stop=True))
            mmn = ct.n
            rec.wait("a", ct, mmn)
            for hh in range(HPC):
                A(lambda e, hh=hh: e.activation(
                    colf(hh, 4), pb[3][:, hh * NSUB:(hh + 1) * NSUB],
                    AF.Copy))
            W("a", ca)
            for hh in range(HPC):
                A(lambda e, hh=hh: e.activation(colf(hh, 5), colf(hh, 4),
                                                AF.Copy, scale=-1.0))
                A(lambda e, hh=hh: e.activation(colf(hh, 6), colf(hh, 4),
                                                AF.Exp))
            W("t", ca)
            for hh in range(HPC):
                T(lambda e, hh=hh: e.matmul(
                    pb[3][:, (HPC + hh) * NSUB:(HPC + hh + 1) * NSUB],
                    G.s_e127_s[:, :], colf(hh, 4), start=True, stop=True))
            mmn = ct.n
            rec.wait("a", ct, mmn)
            rec.wait("v", ct, mmn)
            for hh in range(HPC):
                A(lambda e, hh=hh: e.activation(
                    colf(hh, 8), pb[3][:, (HPC + hh) * NSUB:
                                       (HPC + hh + 1) * NSUB], AF.Exp))
                V(lambda e, hh=hh: e.tensor_sub(
                    colf(hh, 12), pb[3][:, (HPC + hh) * NSUB:
                                        (HPC + hh + 1) * NSUB], colf(hh, 4)))
            W("a", cv)
            for hh in range(HPC):
                A(lambda e, hh=hh: e.activation(colf(hh, 7), colf(hh, 12),
                                                AF.Exp))
            W("v", ca)
            for hh in range(HPC):
                V(lambda e, hh=hh: e.tensor_mul(colf(hh, 9), colf(hh, 3),
                                                colf(hh, 6)))
            W("t", cv)
            for hh in range(HPC):
                T(lambda e, hh=hh: e.transpose(
                    pb[4][0:2 * NSUB, 128 + 128 * hh:256 + 128 * hh],
                    mats[:, (hh * MS + 3) * NSUB:(hh * MS + 5) * NSUB],
                    G.s_identf_s[:, :]))
            mmn = ct.n
            rec.wait("a", ct, mmn)
            W("a", crw)
            for hh in range(HPC):
                A(lambda e, hh=hh: e.activation(
                    rbt2[hh][0:2 * NSUB, :],
                    pb[4][0:2 * NSUB, 128 + 128 * hh:256 + 128 * hh],
                    AF.Copy))
            an = ca.n
            rec.wait("s", ca, an)
            for hh in range(HPC):
                roff = ((b * NT + i) * HPC + hh) * 2 * NSUB
                rec.op("s", lambda e, roff=roff, hh=hh: e.dma_start(
                    out=G.rowscr[roff:roff + 2 * NSUB, :],
                    in_=rbt2[hh][0:2 * NSUB, :]), dma=crw)
            wn_ = crw.n
            rec.wait("s", crw, wn_)
            W("s", cg)
            W("s", ct)
            for hh in range(HPC):
                roff = ((b * NT + i) * HPC + hh) * 2 * NSUB
                rec.op("s", lambda e, roff=roff, hh=hh: e.dma_start(
                    out=rowb[0:1, hh * 3 * TQ: hh * 3 * TQ + TQ],
                    in_=G.rowscr[roff:roff + NSUB, :].rearrange(
                        "r k -> (r k)").unsqueeze(0)), dma=crw)
                rec.op("s", lambda e, roff=roff, hh=hh: e.dma_start(
                    out=rowb[0:1, hh * 3 * TQ + TQ: hh * 3 * TQ + 2 * TQ],
                    in_=G.rowscr[roff + NSUB:roff + 2 * NSUB, :].rearrange(
                        "r k -> (r k)").unsqueeze(0)), dma=crw)
            rows_in = crw.n
            sq_done = ct.n
            W("a", cg)
            for hh in range(HPC):
                for part in (0, 1):
                    rec.wait("a", ct, sq_done)
                    W("a", cv)
                    A(lambda e, hh=hh, part=part, par=par: e.activation(
                        rowsq[:, :],
                        qkvs[par][:, (3 * hh + part) * TQ:
                                  (3 * hh + part + 1) * TQ],
                        AF.Square))
                    an2 = ca.n
                    rec.wait("t", ca, an2)
                    T(lambda e: e.matmul(
                        pb[4][0:1, 0:TQ], G.s_ones_col_b[:, :],
                        rowsq[:, 0:TQ], start=True, stop=True))
                    sq_done = ct.n
                    rec.wait("a", ct, sq_done)
                    ro = part * 512
                    if part == 0:
                        bq_ = G.cst(128e-6)[0:1, :]
                        A(lambda e, b_=bq_, ro=ro: e.activation(
                            rowt[0:1, ro:ro + TQ], pb[4][0:1, 0:TQ],
                            AF.Sqrt, scale=128.0, bias=b_))
                    else:
                        bk_ = G.cst(1e-6)[0:1, :]
                        A(lambda e, b_=bk_, ro=ro: e.activation(
                            rowt[0:1, ro:ro + TQ], pb[4][0:1, 0:TQ],
                            AF.Sqrt, bias=b_))
                    an3 = ca.n
                    rec.wait("v", ca, an3)
                    if part == 0:
                        V(lambda e, hh=hh, ro=ro: e.reciprocal(
                            rowft[0:1, hh * TQ:(hh + 1) * TQ],
                            rowt[0:1, ro:ro + TQ]))
                    else:
                        V(lambda e, hh=hh, ro=ro: e.reciprocal(
                            rowbt[0:1, hh * TQ:(hh + 1) * TQ],
                            rowt[0:1, ro:ro + TQ]))
                    vn_ = cv.n
                    if part == 1:
                        rec.wait("t", cv, vn_)
                        for cc8 in range(NSUB):
                            T(lambda e, hh=hh, cc8=cc8: e.matmul(
                                pb[4][:, 2 + hh * NSUB + cc8:
                                      3 + hh * NSUB + cc8],
                                rowbt[0:1, hh * TQ + cc8 * 128:
                                      hh * TQ + (cc8 + 1) * 128],
                                G.s_one11_f[:, :], start=True, stop=True))
                        mmn2 = ct.n
                        rec.wait("a", ct, mmn2)
                        A(lambda e, hh=hh: e.activation(
                            rkcol[:, hh * NSUB:(hh + 1) * NSUB],
                            pb[4][:, 2 + hh * NSUB: 2 + (hh + 1) * NSUB],
                            AF.Copy))
                        rec.wait("t", ca, ca.n)
            W("g", cv)
            W("g", ct)
            for hh in range(HPC):
                P(lambda e, hh=hh: e.tensor_copy(
                    rowbb[0:1, hh * 3 * TQ: hh * 3 * TQ + TQ],
                    rowft[0:1, hh * TQ:(hh + 1) * TQ]))
            rec.wait("a", crw, rows_in)
            for hh in range(HPC):
                A(lambda e, hh=hh: e.activation(
                    rowb[0:1, hh * 3 * TQ + 2 * TQ: hh * 3 * TQ + 3 * TQ],
                    rowb[0:1, hh * 3 * TQ + TQ: hh * 3 * TQ + 2 * TQ],
                    AF.Exp))
            an4 = ca.n
            rec.wait("g", ca, an4)
            rec.wait("g", crw, rows_in)
            for hh in range(HPC):
                P(lambda e, hh=hh: e.tensor_mul(
                    rowbb[0:1, hh * 3 * TQ + TQ: hh * 3 * TQ + 2 * TQ],
                    rowb[0:1, hh * 3 * TQ: hh * 3 * TQ + TQ],
                    rowbt[0:1, hh * TQ:(hh + 1) * TQ]))
                P(lambda e, hh=hh: e.tensor_mul(
                    rowbb[0:1, hh * 3 * TQ + 2 * TQ: hh * 3 * TQ + 3 * TQ],
                    rowb[0:1, hh * 3 * TQ + 2 * TQ: hh * 3 * TQ + 3 * TQ],
                    rowft[0:1, hh * TQ:(hh + 1) * TQ]))
            setr("pb3", ca, cv)
            setr("pb4", ca, cv)
            setr("rows", ca, cg, cv)

        def prescan_gen(b, i, par, c8, hh):
            p2 = c8 % 2
            sl2 = p2 * HPC + hh
            base = hh * MS * NSUB
            col1 = lambda s, base=base, c8=c8: mats[:, base + s * NSUB + c8:
                                                    base + s * NSUB + c8 + 1]
            qo = (3 * hh + 0) * TQ + c8 * 128
            ko = (3 * hh + 1) * TQ + c8 * 128
            qT_c = qkvs[par][:, qo:qo + 128]
            kT_c = qkvs[par][:, ko:ko + 128]
            ktm_c = ktm[par][:, hh * TQ + c8 * 128: hh * TQ + (c8 + 1) * 128]
            vtm_c = vtm[par][:, hh * TQ + c8 * 128: hh * TQ + (c8 + 1) * 128]
            bank = DB[sl2]
            btr = DBtr[sl2]
            bname = f"dbank{sl2}"
            slname = f"slot{sl2}"
            shname = f"shared{sl2}"
            rkc = rkcol[:, hh * NSUB + c8: hh * NSUB + c8 + 1]
            rrow = lambda s: rowb[0:1, hh * 3 * TQ + s * TQ + c8 * 128:
                                  hh * 3 * TQ + s * TQ + (c8 + 1) * 128]
            brow = lambda s: rowbb[0:1, hh * 3 * TQ + s * TQ + c8 * 128:
                                   hh * 3 * TQ + s * TQ + (c8 + 1) * 128]
            so = sl2 * 128
            so2 = sl2 * 256
            mo_ = sl2 * 128
            eo = sl2 * 256
            uo = sl2 * NLV * 128
            to = sl2 * 256
            xo0 = sl2 * 512
            ao = sl2 * 256
            use("t", bname, "rows")
            T(lambda e: e.matmul(bank[:, 0:128], G.s_ones_row_f[:1, :128],
                                 rrow(1), start=True, stop=True))
            T(lambda e: e.matmul(bank[:, 128:256], G.s_ones_row_b[:1, :128],
                                 brow(0), start=True, stop=True))
            T(lambda e: e.matmul(bank[:, 256:384], G.s_ones_row_b[:1, :128],
                                 brow(1), start=True, stop=True))
            T(lambda e: e.matmul(bank[:, 384:512], G.s_ones_row_b[:1, :128],
                                 brow(2), start=True, stop=True))
            use("t", "pb4")
            T(lambda e: e.matmul(pb[4][:, 0:128], kT_c, qT_c,
                                 start=True, stop=True))
            T(lambda e: e.matmul(pb[4][:, 128:256], kT_c, kT_c,
                                 start=True, stop=True))
            au_mm = ct.n
            use("a", shname)
            rec.wait("a", ct, au_mm)
            A(lambda e: e.activation(rawAU[:, ao:ao + 128], pb[4][:, 0:128],
                                     AF.Copy))
            A(lambda e: e.activation(rawAU[:, ao + 128:ao + 256],
                                     pb[4][:, 128:256], AF.Copy))
            setr("pb4", ca)
            bc_done = ct.n
            yield
            use("v", shname)
            rec.wait("v", ct, bc_done)
            V(lambda e: e.tensor_add(Mb[:, mo_:mo_ + 128], bank[:, 0:128],
                                     G.s_mask_us[:, :]))
            v1_ = cv.n
            rec.wait("a", cv, v1_)
            A(lambda e: e.activation(Eb[:, eo:eo + 128], Mb[:, mo_:mo_ + 128],
                                     AF.Exp, bias=col1(5)))
            a1_ = ca.n
            rec.wait("v", ca, a1_)
            V(lambda e: e.tensor_add(Mb[:, mo_:mo_ + 128], bank[:, 0:128],
                                     G.s_mask_ui[:, :]))
            v2_ = cv.n
            rec.wait("a", cv, v2_)
            A(lambda e: e.activation(Eb[:, eo + 128:eo + 256],
                                     Mb[:, mo_:mo_ + 128], AF.Exp,
                                     bias=col1(5)))
            a2_ = ca.n
            yield
            use("v", slname)
            rec.wait("v", ca, a2_)
            V(lambda e: e.scalar_tensor_tensor(
                mUb[:, mo_:mo_ + 128], Eb[:, eo:eo + 128], rkc,
                bank[:, 256:384], ALU.mult, ALU.mult))
            V(lambda e: e.scalar_tensor_tensor(
                mab[:, mo_:mo_ + 128], Eb[:, eo + 128:eo + 256], rkc,
                bank[:, 128:256], ALU.mult, ALU.mult))
            v3_ = cv.n
            rec.wait("v", cv, v3_)
            V(lambda e: e.tensor_mul(Upow[:, uo:uo + 128],
                                     rawAU[:, ao + 128:ao + 256],
                                     mUb[:, mo_:mo_ + 128]))
            V(lambda e: e.tensor_mul(attT[:, so:so + 128],
                                     rawAU[:, ao:ao + 128],
                                     mab[:, mo_:mo_ + 128]))
            V(lambda e: e.tensor_mul(qgT[:, so:so + 128], qT_c,
                                     bank[:, 384:512]))
            ub_done = cv.n
            yield
            rec.wait("t", cv, ub_done)
            T(lambda e: e.transpose(btr[:, 0:128], Upow[:, uo:uo + 128],
                                    G.s_identb[:, :]))
            t4 = ct.n
            rec.wait("a", ct, t4)
            A(lambda e: e.activation(Tpow[:, to:to + 128], btr[:, 0:128],
                                     AF.Copy))
            lvla = ca.n
            yield
            for lv in range(1, NLV):
                up = Upow[:, uo + (lv - 1) * 128: uo + lv * 128]
                tp = Tpow[:, to + ((lv - 1) % 2) * 128:
                          to + ((lv - 1) % 2 + 1) * 128]
                rec.wait("t", ca, lvla)
                T(lambda e, up=up, tp=tp: e.matmul(
                    bank[:, 0:128], tp, up, start=True, stop=True))
                if lv < NLV - 1:
                    T(lambda e, up=up, tp=tp: e.matmul(
                        bank[:, 128:256], up, tp, start=True, stop=True))
                mmn = ct.n
                rec.wait("a", ct, mmn)
                A(lambda e, lv=lv: e.activation(
                    Upow[:, uo + lv * 128:uo + (lv + 1) * 128],
                    bank[:, 0:128], AF.Copy))
                if lv < NLV - 1:
                    A(lambda e, lv=lv: e.activation(
                        Tpow[:, to + (lv % 2) * 128:to + (lv % 2 + 1) * 128],
                        bank[:, 128:256], AF.Copy))
                lvla = ca.n
                if lv % 2 == 0:
                    yield
            yield
            use("g", slname, shname, "rows")
            W("g", cg)
            P(lambda e: e.tensor_mul(colsc[:, sl2 * 8 + 1:sl2 * 8 + 2],
                                     col1(9), rkc))
            P(lambda e: e.tensor_mul(colsc[:, sl2 * 8 + 2:sl2 * 8 + 3],
                                     col1(7), rkc))
            g1_ = cg.n
            rec.wait("g", cg, g1_)
            P(lambda e: e.tensor_scalar_mul(Xb[:, xo0:xo0 + 128], vtm_c,
                                            col1(2)))
            P(lambda e: e.tensor_scalar_mul(
                Xb[:, xo0 + 128:xo0 + 256], ktm_c,
                colsc[:, sl2 * 8 + 1:sl2 * 8 + 2]))
            P(lambda e: e.tensor_scalar_mul(
                kdec[:, so:so + 128], ktm_c,
                colsc[:, sl2 * 8 + 2:sl2 * 8 + 3]))
            g2_ = cg.n
            yield
            rec.wait("t", ca, lvla)
            rec.wait("t", cg, g2_)
            prev_add = 0
            for lv in range(NLV):
                xs = xo0 + (lv % 2) * 256
                xd = xo0 + ((lv + 1) % 2) * 256
                rec.wait("t", cv, prev_add)
                T(lambda e, lv=lv, xs=xs: e.matmul(
                    bank[:, 0:256],
                    Upow[:, uo + lv * 128:uo + (lv + 1) * 128],
                    Xb[:, xs:xs + 256], start=True, stop=True))
                mmn = ct.n
                rec.wait("v", ct, mmn)
                if lv < NLV - 1:
                    V(lambda e, xs=xs, xd=xd: e.tensor_add(
                        Xb[:, xd:xd + 256], bank[:, 0:256],
                        Xb[:, xs:xs + 256]))
                else:
                    V(lambda e, xs=xs: e.tensor_add(
                        Xfin[:, so2:so2 + 256], bank[:, 0:256],
                        Xb[:, xs:xs + 256]))
                prev_add = cv.n
                if lv == 3:
                    yield
            yield
            use("a", slname)
            rec.wait("t", cv, prev_add)
            T(lambda e: e.transpose(btr[:, 0:128],
                                    Xfin[:, so2 + 128:so2 + 256],
                                    G.s_identb[:, :]))
            t8 = ct.n
            rec.wait("a", ct, t8)
            A(lambda e: e.activation(wnT[:, so:so + 128], btr[:, 0:128],
                                     AF.Copy))
            setr(bname, ca, cv)
            setr(shname, ct, cv, ca, cg)
            setr(f"pres{sl2}", ct, cv, ca, cg)

        def scan_gen(b, i, par, c8):
            p2 = c8 % 2
            for hh in range(HPC):
                sl2 = p2 * HPC + hh
                base = hh * MS * NSUB
                col1 = lambda s, base=base, c8=c8: mats[
                    :, base + s * NSUB + c8: base + s * NSUB + c8 + 1]
                so = sl2 * 128
                so2 = sl2 * 256
                ho = hh * 128
                Sb_h = Sb[:, ho:ho + 128]
                Sf_h = Sf[:, ho:ho + 128]
                ssn = f"sst{hh}"
                use("t", f"pres{sl2}", ssn, "pb7")
                T(lambda e, so=so, Sb_h=Sb_h: e.matmul(
                    pb[7][:, 0:128], wnT[:, so:so + 128], Sb_h,
                    start=True, stop=True))
                vready = ct.n
                rec.wait("v", ct, vready)
                use("v", ssn)
                V(lambda e, so2=so2, ho=ho: e.tensor_add(
                    vnew[:, ho:ho + 128], pb[7][:, 0:128],
                    Xfin[:, so2:so2 + 128]))
                vn_done = cv.n
                yield
                use("t", "pb0")
                T(lambda e, so=so, Sb_h=Sb_h: e.matmul(
                    pb[0][:, 0:128], qgT[:, so:so + 128], Sb_h,
                    start=True, stop=False))
                rec.wait("t", cv, vn_done)
                T(lambda e, so=so, ho=ho: e.matmul(
                    pb[0][:, 0:128], attT[:, so:so + 128],
                    vnew[:, ho:ho + 128], start=False, stop=True))
                T(lambda e, so=so, ho=ho: e.matmul(
                    pb[7][:, 128:256], kdec[:, so:so + 128],
                    vnew[:, ho:ho + 128], start=True, stop=True))
                smm = ct.n
                rec.wait("v", ct, smm)
                V(lambda e, Sf_h=Sf_h, col1=col1: e.scalar_tensor_tensor(
                    Sf_h, Sf_h, col1(8), pb[7][:, 128:256],
                    ALU.mult, ALU.add))
                setr("pb7", cv)
                sup = cv.n
                rec.wait("a", cv, sup)
                A(lambda e, Sb_h=Sb_h, Sf_h=Sf_h: e.activation(
                    Sb_h, Sf_h, AF.Copy))
                yield
                A(lambda e, ho=ho: e.activation(osq[:, ho:ho + 128],
                                                pb[0][:, 0:128], AF.Square))
                osq_a = ca.n
                rec.wait("v", ca, osq_a)
                go = hh * 4
                V(lambda e, go=go, ho=ho: e.reduce_sum(
                    oc2[:, go:go + 1], osq[:, ho:ho + 128], AX.X))
                rr1 = cv.n
                rec.wait("a", cv, rr1)
                b_e6 = G.cst(1e-6)
                A(lambda e, b=b_e6, go=go: e.activation(
                    oc2[:, go + 1:go + 2], oc2[:, go:go + 1], AF.Sqrt,
                    scale=1.0 / 128, bias=b))
                rr2 = ca.n
                rec.wait("v", ca, rr2)
                V(lambda e, go=go: e.reciprocal(oc2[:, go + 2:go + 3],
                                                oc2[:, go + 1:go + 2]))
                rr3 = cv.n
                rec.wait("v", cv, rr3)
                V(lambda e, go=go, ho=ho: e.scalar_tensor_tensor(
                    ot1[:, ho:ho + 128], pb[0][:, 0:128],
                    oc2[:, go + 2:go + 3], G.s_wnorm_bc[:, :],
                    ALU.mult, ALU.mult))
                setr("pb0", cv, ca)
                use("v", "onorm")
                rr4 = cv.n
                rec.wait("v", cv, rr4)
                V(lambda e, par=par, c8=c8, ho=ho: e.tensor_mul(
                    onorm[:, ho:ho + 128], ot1[:, ho:ho + 128],
                    zsil[par][:, c8 * HPC * 128 + ho:
                              c8 * HPC * 128 + ho + 128]))
                setr(ssn, ct, cv, ca)
                yield

        def store_chunk(b, i, c8):
            W("s", cv)
            r0 = b * c.S + i * TQ + c8 * 128
            rec.op("s", lambda e, r0=r0: e.dma_start(
                out=G.agin_na[r0:r0 + 128, :], in_=onorm[:, :]), dma=cso)
            setr("onorm", cso)

        tiles = [(b, i) for b in range(c.B) for i in range(NT)]
        load_hT(0, 0, 0)
        sa = stage_a_gen(0, 0, 0, first=True)
        for _ in sa:
            pass
        if len(tiles) > 1:
            load_hT(tiles[1][0], tiles[1][1], 1)
        sa = None
        for k, (b, i) in enumerate(tiles):
            par = k % 2
            if i == 0:
                W("v", ct)
                W("v", ca)
                V(lambda e: e.memset(Sf[:, :], 0.0))
                V(lambda e: e.memset(Sb[:, :], 0.0))
            mats_prep(b, i, par)
            if k + 1 < len(tiles):
                bn, in_ = tiles[k + 1]
                sa = stage_a_gen(bn, in_, (k + 1) % 2, first=(in_ == 0))
            if k + 2 < len(tiles):
                b2, i2 = tiles[k + 2]
                use("s", f"hTfree{k % 2}")
                load_hT(b2, i2, k % 2)
            for c8 in range(NSUB + 1):
                gens = []
                if c8 < NSUB:
                    gens += [prescan_gen(b, i, par, c8, hh)
                             for hh in range(HPC)]
                if c8 > 0:
                    gens.append(scan_gen(b, i, par, c8 - 1))
                while gens:
                    for g_ in list(gens):
                        try:
                            next(g_)
                        except StopIteration:
                            gens.remove(g_)
                    if sa is not None:
                        try:
                            next(sa)
                        except StopIteration:
                            sa = None
                if c8 > 0:
                    cp = c8 - 1
                    for hh in range(HPC):
                        setr(f"slot{(cp % 2) * HPC + hh}", ct)
                    store_chunk(b, i, cp)
            while sa is not None:
                try:
                    next(sa)
                except StopIteration:
                    sa = None
            setr(f"tdone{par}", ct, cv, ca, cg)
        W("g", cso)

    rec.op("g", lambda e: e.collective_compute(
        "AllGather", ALU.bypass, replica_groups=[list(range(c.NC))],
        ins=[G.agin_na[:, :]], outs=[G.ag_na[:, :]]), tick=ch("cc_na"))
    G.dbg("na", G.ag_na, after_ch=ch("cc_na"))


def p_mixer_out(G, tag, ag, my, cc, w_direct, bias, x_src, x_dst, h_dst, dep):
    c, nc, rec, ch = G.c, G.nc, G.rec, G.ch
    Tc, D, KC, HPC = c.Tc, c.D, c.KC, c.HPC
    with ExitStack() as S:
        def sb(n, shape, dt):
            return S.enter_context(nc.sbuf_tensor(tag + "_" + n, list(shape), dt))

        nT = sb("nT", (128, KC * Tc), BF16)
        wsb = sb("w", (128, KC * D), BF16)
        xa = sb("xa", (128, D), F32)
        xb = sb("xb", (128, D), F32)
        xt = sb("xt", (128, D), F32)
        hb = sb("hb", (128, D), BF16)
        sq = sb("sq", (128, D), F32)
        cols = sb("cols", (128, 8), F32)
        bo_sb = sb("bo", (128, D), F32) if bias is not None else None
        bo_row = sb("borow", (1, D), F32) if bias is not None else None
        pp = [S.enter_context(nc.psum_tensor(tag + f"_p{i}", [128, 512], F32))
              for i in range(3)]

        ccp = ch(tag + "_cp")
        ctr = ch(tag + "_tr")
        cwl = ch(tag + "_wl")
        cxl = ch(tag + "_xl")
        cst = ch(tag + "_st")
        ct = ch(tag + "_t")
        cv = ch(tag + "_v")
        ca = ch(tag + "_a")
        W = rec.wait

        if dep is not None:
            W("s", dep)
        W("s", ch(cc))
        rec.op("s", lambda e: e.dma_start(
            out=my.rearrange("t (r x) -> r t x", r=c.NC),
            in_=ag.rearrange("(r t) x -> r t x", r=c.NC)
                [:, bass.ds(e.partition_id() * Tc, Tc), :]), dma=ccp)
        W("s", ccp)
        for dch in range(KC):
            rec.op("s", lambda e, dch=dch: e.dma_start(
                out=nT[:, dch * Tc:(dch + 1) * Tc],
                in_=my[0:Tc, dch * 128:(dch + 1) * 128],
                transpose=True), dma=ctr)
        if w_direct is not None:
            rec.op("g", lambda e: e.dma_start(
                out=wsb[:, :].rearrange("p (ch f) -> p ch f", f=D),
                in_=w_direct.rearrange("(ch p) f -> p ch f", p=128)), dma=cwl)
        else:
            W("a", ch("c_wprep"))
            for ech in range(KC):
                rec.op("a", lambda e, ech=ech: e.dma_start(
                    out=wsb[:, ech * D:(ech + 1) * D],
                    in_=G.wo_b[0:D, ech * 128:(ech + 1) * 128],
                    transpose=True), dma=cwl)
        if bias is not None:
            cbl = ch(tag + "_bl")
            rec.op("s", lambda e: e.dma_start(out=bo_row[:, :], in_=bias),
                   dma=cbl)
            W("t", cbl)
            for f in range(D // 512):
                rec.op("t", lambda e, f=f: e.matmul(
                    pp[2][:, 0:512], G.s_ones_row_f[:1, :128],
                    bo_row[0:1, f * 512:(f + 1) * 512],
                    start=True, stop=True), tick=ct)
                W("v", ct)
                rec.op("v", lambda e, f=f: e.tensor_copy(
                    bo_sb[:, f * 512:(f + 1) * 512], pp[2][:, 0:512]),
                    tick=cv)
                W("t", cv)

        b_eps = G.cst(1e-5)
        for tt in range(Tc // 128):
            xo = (xa, xb)[tt % 2]
            rec.op("s", lambda e, tt=tt, xo=xo: e.dma_start(
                out=xo[:, :], in_=x_src[tt * 128:(tt + 1) * 128, :]), dma=cxl)
            for f in range(D // 512):
                ps = pp[f % 2]
                W("t", ctr)
                W("t", cwl)
                W("t", cv)
                for dch in range(KC):
                    rec.op("t", lambda e, tt=tt, f=f, dch=dch, ps=ps: e.matmul(
                        ps[:, 0:512],
                        nT[:, dch * Tc + tt * 128: dch * Tc + (tt + 1) * 128],
                        wsb[:, dch * D + f * 512: dch * D + (f + 1) * 512],
                        start=(dch == 0), stop=(dch == KC - 1)), tick=ct)
                W("v", ct)
                W("v", cxl)
                W("v", ca)
                rec.op("v", lambda e, f=f, ps=ps, xo=xo: e.tensor_add(
                    xt[:, f * 512:(f + 1) * 512], ps[:, 0:512],
                    xo[:, f * 512:(f + 1) * 512]), tick=cv)
                if bias is not None:
                    W("v", cv)
                    rec.op("v", lambda e, f=f: e.tensor_add(
                        xt[:, f * 512:(f + 1) * 512],
                        xt[:, f * 512:(f + 1) * 512],
                        bo_sb[:, f * 512:(f + 1) * 512]), tick=cv)
            W("v", cv)
            W("v", cst)
            rec.op("v", lambda e: e.reduce_sum(cols[:, 0:1], xt[:, :], AX.X),
                   tick=cv)
            W("a", cv)
            rec.op("a", lambda e: e.activation(cols[:, 1:2], cols[:, 0:1],
                                               AF.Copy, scale=1.0 / D), tick=ca)
            W("v", ca)
            rec.op("v", lambda e: e.tensor_scalar_sub(xt[:, :], xt[:, :],
                                                      cols[:, 1:2]), tick=cv)
            W("a", cv)
            rec.op("a", lambda e: e.activation(sq[:, :], xt[:, :], AF.Square),
                   tick=ca)
            W("v", ca)
            rec.op("v", lambda e: e.reduce_sum(cols[:, 2:3], sq[:, :], AX.X),
                   tick=cv)
            W("a", cv)
            rec.op("a", lambda e, b=b_eps: e.activation(
                cols[:, 3:4], cols[:, 2:3], AF.Sqrt, scale=1.0 / D, bias=b),
                tick=ca)
            W("v", ca)
            rec.op("v", lambda e: e.reciprocal(cols[:, 2:3], cols[:, 3:4]),
                   tick=cv)
            W("v", cv)
            rec.op("v", lambda e: e.tensor_scalar_mul(hb[:, :], xt[:, :],
                                                      cols[:, 2:3]), tick=cv)
            W("v", cv)
            rec.op("v", lambda e: e.tensor_scalar_add(xt[:, :], xt[:, :],
                                                      cols[:, 1:2]), tick=cv)
            W("s", cv)
            rec.op("s", lambda e, tt=tt: e.dma_start(
                out=x_dst[tt * 128:(tt + 1) * 128, :], in_=xt[:, :]), dma=cst)
            rec.op("s", lambda e, tt=tt: e.dma_start(
                out=h_dst[tt * 128:(tt + 1) * 128, :], in_=hb[:, :]), dma=cst)
        W("s", cst)
        W("g", cst)


def p_ffn(G, tag, h_src, x_src, x_dst, h_dst, last, dep):
    c, nc, rec, ch = G.c, G.nc, G.rec, G.ch
    Tc, D, KC, E, F, FS = c.Tc, c.D, c.KC, c.E, c.F, c.FS
    TP = c.FFN_TP
    NP = Tc // TP
    NTS = TP // 128
    FH = max(128, F // 2)
    FSH = min(512, FS)
    FMX = max(FH, FSH)
    with ExitStack() as S:
        def sb(n, shape, dt):
            return S.enter_context(nc.sbuf_tensor(tag + "_" + n, list(shape), dt))

        xT = sb("xT", (128, KC * TP), BF16)
        rout = sb("rout", (128, KC * E), BF16)
        gates = sb("gates", (128, NTS * E), F32)
        sc = sb("sc", (128, E), F32)
        sc2 = sb("sc2", (128, E), F32)
        sc3 = sb("sc3", (128, E), F32)
        gcol = sb("gcol", (128, 8), F32)
        w1s = [sb("w1a", (128, KC * FMX), BF16), sb("w1b", (128, KC * FMX), BF16)]
        v1s = [sb("v1a", (128, KC * FMX), BF16), sb("v1b", (128, KC * FMX), BF16)]
        w2s = [sb("w2a", (128, (FMX // 128) * D), BF16),
               sb("w2b", (128, (FMX // 128) * D), BF16)]
        heT = sb("heT", (128, (FMX // 128) * TP), BF16)
        st = sb("st", (128, 1024), F32)
        acc = sb("acc", (128, NTS * D), F32)
        xa = sb("xa", (128, D), F32)
        xb = sb("xb", (128, D), F32)
        xt = sb("xt", (128, D), F32)
        hb = sb("hb", (128, D), BF16)
        sq = sb("sq", (128, D), F32)
        cols = sb("cols", (128, 8), F32)
        pp = [S.enter_context(nc.psum_tensor(tag + f"_p{i}", [128, 512], F32))
              for i in range(8)]

        ctr = ch(tag + "_tr")
        ctr2 = ch(tag + "_tr2")
        cwl = [ch(tag + "_wl0"), ch(tag + "_wl1")]
        cvl = [ch(tag + "_vl0"), ch(tag + "_vl1")]
        cwa = [ch(tag + "_wla0"), ch(tag + "_wla1")]
        cxl = ch(tag + "_xl")
        cst = ch(tag + "_st")
        ct = ch(tag + "_t")
        cv = ch(tag + "_v")
        ca = ch(tag + "_a")
        W = rec.wait

        def T(fn):
            rec.op("t", fn, tick=ct)

        def V(fn):
            rec.op("v", fn, tick=cv)

        def A(fn):
            rec.op("a", fn, tick=ca)

        W("a", ch("c_wprep"))
        W("g", ch("c_wprep"))
        W("s", ch("c_wprep"))
        crt = ch(tag + "_rt")
        rec.op("a", lambda e: e.dma_start(
            out=rout[:, :].rearrange("p (ch f) -> p ch f", f=E),
            in_=G.routwT_b.rearrange("(ch p) f -> p ch f", p=128)), dma=crt)

        sets = [("e", e, fh) for e in range(E) for fh in range(F // FH)] + \
               [("s", 0, fh) for fh in range(FS // FSH)]

        def load_set(si, buf, part="all"):
            kind, e, fh = sets[si]
            w1v1 = part in ("all", "w1v1")
            w2p = part in ("all", "w2")
            if kind == "e":
                if w1v1:
                    rec.op("g", lambda e_, e=e, fh=fh, buf=buf: e_.dma_start(
                        out=w1s[buf][:, 0:KC * FH].rearrange(
                            "p (ch f) -> p ch f", f=FH),
                        in_=G.w1_b[e].rearrange("(ch p) f -> p ch f", p=128)
                            [:, :, fh * FH:(fh + 1) * FH]), dma=cwl[buf])
                    rec.op("s", lambda e_, e=e, fh=fh, buf=buf: e_.dma_start(
                        out=v1s[buf][:, 0:KC * FH].rearrange(
                            "p (ch f) -> p ch f", f=FH),
                        in_=G.v1_b[e].rearrange("(ch p) f -> p ch f", p=128)
                            [:, :, fh * FH:(fh + 1) * FH]), dma=cvl[buf])
                if w2p:
                    rec.op("g", lambda e_, e=e, fh=fh, buf=buf: e_.dma_start(
                        out=w2s[buf][:, 0:(FH // 128) * D].rearrange(
                            "p (fc h) -> p fc h", h=D),
                        in_=G.w2_b[e, fh * FH:(fh + 1) * FH].rearrange(
                            "(fc p) h -> p fc h", p=128)), dma=cwl[buf])
            else:
                if w1v1:
                    for ech in range(KC):
                        rec.op("a", lambda e_, fh=fh, buf=buf, ech=ech:
                               e_.dma_start(
                            out=w1s[buf][:, ech * FSH:(ech + 1) * FSH],
                            in_=G.sg_b[fh * FSH:(fh + 1) * FSH,
                                       ech * 128:(ech + 1) * 128],
                            transpose=True), dma=cwa[buf])
                        rec.op("a", lambda e_, fh=fh, buf=buf, ech=ech:
                               e_.dma_start(
                            out=v1s[buf][:, ech * FSH:(ech + 1) * FSH],
                            in_=G.su_b[fh * FSH:(fh + 1) * FSH,
                                       ech * 128:(ech + 1) * 128],
                            transpose=True), dma=cwa[buf])
                if w2p:
                    for fc in range(FSH // 128):
                        rec.op("a", lambda e_, fh=fh, buf=buf, fc=fc:
                               e_.dma_start(
                            out=w2s[buf][:, fc * D:(fc + 1) * D],
                            in_=G.sd_b[0:D, fh * FSH + fc * 128:
                                       fh * FSH + (fc + 1) * 128],
                            transpose=True), dma=cwa[buf])

        load_set(0, 0)
        load_set(1, 1)
        W("a", dep)
        W("s", dep)
        wdone = [0, 0]
        xt_free = [0]

        for pp_i in range(NP):
            t0 = pp_i * TP
            rec.wait("a", ct, xt_free[0])
            for ech in range(KC):
                rec.op("a", lambda e, ech=ech, t0=t0: e.dma_start(
                    out=xT[:, ech * TP:(ech + 1) * TP],
                    in_=h_src[t0:t0 + TP, ech * 128:(ech + 1) * 128],
                    transpose=True), dma=ctr)
            W("t", ctr)
            W("t", crt)
            for ts in range(NTS):
                W("t", cv)
                for ech in range(KC):
                    T(lambda e, ts=ts, ech=ech: e.matmul(
                        pp[2][:, 0:E],
                        xT[:, ech * TP + ts * 128: ech * TP + (ts + 1) * 128],
                        rout[:, ech * E:(ech + 1) * E],
                        start=(ech == 0), stop=(ech == KC - 1)))
                W("v", ct)
                V(lambda e: e.tensor_copy(sc[:, :], pp[2][:, 0:E]))
                if tag == "f1" and "scores" in G.debug:
                    W("s", cv)
                    rec.op("s", lambda e, ts=ts, t0=t0: e.dma_start(
                        out=G.dbg_scores[t0 + ts * 128:t0 + (ts + 1) * 128, :],
                        in_=sc[:, :]), dma=ch("c_dbgs"))
                    rec.wait("v", ch("c_dbgs"))
                W("v", cv)
                V(lambda e: e.reduce_max(gcol[:, 0:1], sc[:, :], AX.X))
                for r in range(3):
                    cur = sc if r == 0 else sc2
                    mc = gcol[:, 0:1] if r == 0 else gcol[:, 3:4]
                    W("v", cv)
                    V(lambda e, cur=cur, mc=mc: e.tensor_scalar(
                        sc3[:, :], cur[:, :], mc, None, ALU.is_equal))
                    W("v", cv)
                    V(lambda e, cur=cur: e.scalar_tensor_tensor(
                        sc2[:, :], sc3[:, :], NEG, cur[:, :],
                        ALU.mult, ALU.add))
                    W("v", cv)
                    V(lambda e: e.reduce_max(gcol[:, 3:4], sc2[:, :], AX.X))
                W("v", cv)
                V(lambda e: e.tensor_scalar(gcol[:, 5:6], gcol[:, 0:1], -1.0,
                                            None, ALU.mult))
                W("a", cv)
                A(lambda e: e.activation(sc[:, :], sc[:, :], AF.Exp,
                                         bias=gcol[:, 5:6]))
                W("a", ca)
                A(lambda e: e.activation(gcol[:, 3:4], gcol[:, 3:4], AF.Exp,
                                         bias=gcol[:, 5:6]))
                W("v", ca)
                V(lambda e: e.tensor_scalar(sc3[:, :], sc[:, :],
                                            gcol[:, 3:4], None, ALU.is_ge))
                W("v", cv)
                V(lambda e: e.tensor_mul(sc[:, :], sc[:, :], sc3[:, :]))
                W("v", cv)
                V(lambda e: e.reduce_sum(gcol[:, 6:7], sc[:, :], AX.X))
                W("v", cv)
                V(lambda e: e.reciprocal(gcol[:, 7:8], gcol[:, 6:7]))
                W("v", cv)
                V(lambda e, ts=ts: e.tensor_scalar_mul(
                    gates[:, ts * E:(ts + 1) * E], sc[:, :], gcol[:, 7:8]))
                if tag == "f1" and "gates" in G.debug:
                    W("s", cv)
                    rec.op("s", lambda e, ts=ts, t0=t0: e.dma_start(
                        out=G.dbg_gates[t0 + ts * 128:t0 + (ts + 1) * 128, :],
                        in_=gates[:, ts * E:(ts + 1) * E]), dma=ch("c_dbgg"))
            W("v", cv)
            V(lambda e: e.memset(acc[:, :], 0.0))
            heb = [(0, 1), (4, 5)]
            yb = [2, 3, 6, 7]
            he_done = {0: (0, 0), 1: (0, 0)}
            y_done = {k: 0 for k in yb}
            NH = D // 512
            assert NH <= 4
            for si in range(len(sets)):
                buf = si % 2
                kind, e_i, fh = sets[si]
                FL = FH if kind == "e" else FSH
                NF = FL // 128
                W("t", cwl[buf])
                W("t", cvl[buf])
                W("t", cwa[buf])
                for fc in range(NF):
                    par2 = fc % 2
                    pA, pB = pp[heb[par2][0]], pp[heb[par2][1]]
                    rec.wait("t", ca, he_done[par2][0])
                    rec.wait("t", cv, he_done[par2][1])
                    for nh in range(max(1, TP // 512)):
                        nn = min(512, TP)
                        for ech in range(KC):
                            T(lambda e, fc=fc, nh=nh, ech=ech, buf=buf,
                              FL=FL, nn=nn, pA=pA: e.matmul(
                                pA[:, 0:nn],
                                w1s[buf][:, ech * FL + fc * 128:
                                         ech * FL + (fc + 1) * 128],
                                xT[:, ech * TP + nh * 512:
                                   ech * TP + nh * 512 + nn],
                                start=(ech == 0), stop=(ech == KC - 1)))
                        for ech in range(KC):
                            T(lambda e, fc=fc, nh=nh, ech=ech, buf=buf,
                              FL=FL, nn=nn, pB=pB: e.matmul(
                                pB[:, 0:nn],
                                v1s[buf][:, ech * FL + fc * 128:
                                         ech * FL + (fc + 1) * 128],
                                xT[:, ech * TP + nh * 512:
                                   ech * TP + nh * 512 + nn],
                                start=(ech == 0), stop=(ech == KC - 1)))
                        W("a", ct)
                        W("a", cv)
                        A(lambda e, nn=nn, pA=pA, par2=par2: e.activation(
                            st[:, par2 * 512:par2 * 512 + nn],
                            pA[:, 0:nn], AF.Silu))
                        W("v", ca)
                        V(lambda e, fc=fc, nh=nh, nn=nn, pB=pB, par2=par2:
                          e.tensor_mul(
                            heT[:, fc * TP + nh * 512:
                                fc * TP + nh * 512 + nn],
                            st[:, par2 * 512:par2 * 512 + nn],
                            pB[:, 0:nn]))
                    he_done[par2] = (ca.n, cv.n)
                xt_free[0] = ct.n
                nsi = si + 2
                do_pf = nsi < len(sets) or pp_i + 1 < NP
                if nsi >= len(sets):
                    nsi -= len(sets)
                if do_pf:
                    he_w_done = ct.n
                    rec.wait("g", ct, he_w_done)
                    rec.wait("a", ct, he_w_done)
                    rec.wait("s", ct, he_w_done)
                    load_set(nsi, buf, part="w1v1")
                he_ready = cv.n
                for tt in range(NTS):
                    rec.wait("t", cv, he_ready)
                    for hc in range(NH):
                        rec.wait("t", cv, y_done[yb[hc]])
                    for fc in range(NF):
                        for hc in range(NH):
                            T(lambda e, tt=tt, hc=hc, fc=fc, buf=buf, NF=NF,
                              py=pp[yb[hc]]: e.matmul(
                                py[:, 0:512],
                                heT[:, fc * TP + tt * 128:
                                    fc * TP + (tt + 1) * 128],
                                w2s[buf][:, fc * D + hc * 512:
                                         fc * D + (hc + 1) * 512],
                                start=(fc == 0), stop=(fc == NF - 1)))
                    ymm_done = ct.n
                    for hc in range(NH):
                        rec.wait("v", ct, ymm_done)
                        rec.wait("v", cv, y_done[yb[hc]])
                        if kind == "e":
                            V(lambda e, tt=tt, hc=hc, e_i=e_i,
                              py=pp[yb[hc]]: e.scalar_tensor_tensor(
                                acc[:, tt * D + hc * 512:
                                    tt * D + (hc + 1) * 512],
                                py[:, 0:512],
                                gates[:, tt * E + e_i: tt * E + e_i + 1],
                                acc[:, tt * D + hc * 512:
                                    tt * D + (hc + 1) * 512],
                                ALU.mult, ALU.add))
                        else:
                            V(lambda e, tt=tt, hc=hc, py=pp[yb[hc]]:
                              e.tensor_add(
                                acc[:, tt * D + hc * 512:
                                    tt * D + (hc + 1) * 512],
                                acc[:, tt * D + hc * 512:
                                    tt * D + (hc + 1) * 512],
                                py[:, 0:512]))
                        y_done[yb[hc]] = cv.n
                wdone[buf] = cv.n
                if do_pf:
                    y_w_done = ct.n
                    rec.wait("g", ct, y_w_done)
                    rec.wait("a", ct, y_w_done)
                    load_set(nsi, buf, part="w2")
            b_eps = G.cst(1e-5)
            for tt in range(NTS):
                xo = (xa, xb)[tt % 2]
                W("s", ca)
                W("s", cv)
                rec.op("s", lambda e, tt=tt, t0=t0, xo=xo: e.dma_start(
                    out=xo[:, :], in_=x_src[t0 + tt * 128:t0 + (tt + 1) * 128, :]),
                    dma=cxl)
                W("v", cxl)
                W("v", ca)
                W("v", cst)
                V(lambda e, tt=tt, xo=xo: e.tensor_add(
                    xt[:, :], acc[:, tt * D:(tt + 1) * D], xo[:, :]))
                if last:
                    W("s", cv)
                    rec.op("s", lambda e, tt=tt, t0=t0: e.dma_start(
                        out=x_dst[t0 + tt * 128:t0 + (tt + 1) * 128, :],
                        in_=xt[:, :]), dma=cst)
                else:
                    W("v", cv)
                    V(lambda e: e.reduce_sum(cols[:, 0:1], xt[:, :], AX.X))
                    W("a", cv)
                    A(lambda e: e.activation(cols[:, 1:2], cols[:, 0:1],
                                             AF.Copy, scale=1.0 / D))
                    W("v", ca)
                    V(lambda e: e.tensor_scalar_sub(xt[:, :], xt[:, :],
                                                    cols[:, 1:2]))
                    W("a", cv)
                    A(lambda e: e.activation(sq[:, :], xt[:, :], AF.Square))
                    W("v", ca)
                    V(lambda e: e.reduce_sum(cols[:, 2:3], sq[:, :], AX.X))
                    W("a", cv)
                    A(lambda e, b=b_eps: e.activation(
                        cols[:, 3:4], cols[:, 2:3], AF.Sqrt, scale=1.0 / D,
                        bias=b))
                    W("v", ca)
                    V(lambda e: e.reciprocal(cols[:, 2:3], cols[:, 3:4]))
                    W("v", cv)
                    V(lambda e: e.tensor_scalar_mul(hb[:, :], xt[:, :],
                                                    cols[:, 2:3]))
                    W("v", cv)
                    V(lambda e: e.tensor_scalar_add(xt[:, :], xt[:, :],
                                                    cols[:, 1:2]))
                    W("s", cv)
                    rec.op("s", lambda e, tt=tt, t0=t0: e.dma_start(
                        out=x_dst[t0 + tt * 128:t0 + (tt + 1) * 128, :],
                        in_=xt[:, :]), dma=cst)
                    rec.op("s", lambda e, tt=tt, t0=t0: e.dma_start(
                        out=h_dst[t0 + tt * 128:t0 + (tt + 1) * 128, :],
                        in_=hb[:, :]), dma=cst)
        W("s", cst)
        W("g", cst)


def p_attn_b(G):
    c, nc, rec, ch = G.c, G.nc, G.rec, G.ch
    HPC, KC, S_, D = c.HPC, c.KC, c.S, c.D
    TQ = c.TQ
    NT = S_ // TQ
    NKT = S_ // 128
    with ExitStack() as S:
        def sb(n, shape, dt):
            return S.enter_context(nc.sbuf_tensor("ab_" + n, list(shape), dt))

        hT = [sb("hT0", (128, KC * TQ), BF16), sb("hT1", (128, KC * TQ), BF16)]
        wqT = sb("wqT", (128, KC * HPC * 128), BF16)
        wkT = sb("wkT", (128, KC * HPC * 128), BF16)
        wvT = sb("wvT", (128, KC * HPC * 128), BF16)
        qT = sb("qT", (128, HPC * S_), BF16)
        kT = sb("kT", (128, HPC * S_), BF16)
        vb = sb("vb", (128, HPC * NKT * 132), BF16)
        cosb = sb("cosb", (128, S_), F32)
        sinb = sb("sinb", (128, S_), F32)
        rp = sb("rp", (128, 128), BF16)
        bqk = sb("bqk", (128, 2 * HPC), F32)
        bvb = sb("bvb", (128, HPC * 128), F32)
        brow = sb("brow", (1, 3 * HPC * 128), F32)
        dmask = sb("dmask", (128, 4 * 512), F32)
        rt1 = sb("rt1", (128, 512), F32)
        rt1b = sb("rt1b", (128, 512), BF16)
        rt2 = sb("rt2", (128, 512), F32)
        rt3 = sb("rt3", (128, 512), F32)
        Pb = [sb("Pb0", (128, 512), BF16), sb("Pb1", (128, 512), BF16)]
        onb = sb("onb", (128, 4 * HPC * 128), BF16)
        ocol = sb("ocol", (128, 8), F32)
        pp = [S.enter_context(nc.psum_tensor(f"ab_p{i}", [128, 512], F32))
              for i in range(6)]
        po = [pp[0], pp[1], pp[2], pp[5]]

        cld = ch("cb_ld")
        cwl = ch("cb_wl")
        ct = ch("cb_t")
        cv = ch("cb_v")
        ca = ch("cb_a")
        cg = ch("cb_g")
        cso = ch("cb_so")
        W = rec.wait

        def T(fn):
            rec.op("t", fn, tick=ct)

        def V(fn):
            rec.op("v", fn, tick=cv)

        def A(fn):
            rec.op("a", fn, tick=ca)

        W("a", ch("c_wprep"))
        for ech in range(KC):
            for wsrc, wdst in ((G.wq_b, wqT), (G.wk_b, wkT), (G.wv_b, wvT)):
                rec.op("a", lambda e, ech=ech, ws=wsrc, wd=wdst: e.dma_start(
                    out=wd[:, ech * HPC * 128:(ech + 1) * HPC * 128],
                    in_=ws[0:HPC * 128, ech * 128:(ech + 1) * 128],
                    transpose=True), dma=cwl)
        cws = ch("cb_ws")
        rec.op("s", lambda e: e.dma_start(out=cosb[:, :], in_=G.cosb), dma=cws)
        rec.op("s", lambda e: e.dma_start(out=sinb[:, :], in_=G.sinb), dma=cws)
        rec.op("g", lambda e: e.dma_start(out=rp[:, :], in_=G.ropeperm),
               dma=ch("cb_wg"))
        for r in range(3):
            rec.op("s", lambda e, r=r: e.dma_start(
                out=brow[0:1, r * HPC * 128:(r + 1) * HPC * 128],
                in_=G.bqkv[r:r + 1, :]), dma=cws)
        W("t", cwl)
        W("t", ch("cb_ws"))
        W("t", ch("cb_wg"))
        for qk in range(2):
            for hh in range(HPC):
                T(lambda e, qk=qk, hh=hh: e.matmul(
                    pp[0][:, qk * HPC + hh: qk * HPC + hh + 1],
                    brow[0:1, (qk * HPC + hh) * 128:(qk * HPC + hh + 1) * 128],
                    G.s_one11_f[:, :], start=True, stop=True))
        W("v", ct)
        V(lambda e: e.tensor_copy(bqk[:, :], pp[0][:, 0:2 * HPC]))
        for f in range(0, HPC * 128, 512):
            nn = min(512, HPC * 128 - f)
            W("t", cv)
            T(lambda e, f=f, nn=nn: e.matmul(
                pp[0][:, 0:nn], G.s_ones_row_f[:1, :128],
                brow[0:1, 2 * HPC * 128 + f: 2 * HPC * 128 + f + nn],
                start=True, stop=True))
            W("v", ct)
            V(lambda e, f=f, nn=nn: e.tensor_copy(bvb[:, f:f + nn],
                                                  pp[0][:, 0:nn]))
        for m in range(4):
            rec.op("g", lambda e, m=m: e.memset(
                dmask[:, m * 512:(m + 1) * 512], 0.0), tick=cg)
        W("g", cg)
        for m in range(4):
            rec.op("g", lambda e, m=m: e.affine_select(
                dmask[:, m * 512:(m + 1) * 512],
                dmask[:, m * 512:(m + 1) * 512],
                [[1, 512]], ALU.is_ge, NEG, base=-(m * 128),
                channel_multiplier=-1), tick=cg)

        W("s", ch("cc_h3"))
        b_sh = G.cst(-8.0)

        def load_hT(b, i, par):
            for chk in range(KC):
                rec.op("s", lambda e, b=b, i=i, par=par, chk=chk: e.dma_start(
                    out=hT[par][:, chk * TQ:(chk + 1) * TQ],
                    in_=G.ag_h3[b * S_ + i * TQ: b * S_ + (i + 1) * TQ,
                                chk * 128:(chk + 1) * 128],
                    transpose=True), dma=cld)

        for b in range(c.B):
            if b == 0:
                load_hT(0, 0, 0)
            W("v", ct)
            W("v", ca)
            V(lambda e: e.memset(vb[:, :], 0.0))
            W("v", cv)
            for hh in range(HPC):
                for kt in range(NKT):
                    V(lambda e, hh=hh, kt=kt: e.memset(
                        vb[:, (hh * NKT + kt) * 132 + 128:
                           (hh * NKT + kt) * 132 + 129], 1.0))
            for i in range(NT):
                par = i % 2
                W("t", cld)
                W("t", cg)
                for hh in range(HPC):
                    for qk, (wT, dstT) in enumerate(((wqT, qT), (wkT, kT))):
                        for nh in range(TQ // 512):
                            tpos = i * TQ + nh * 512
                            W("t", cv)
                            W("t", ca)
                            for ech in range(KC):
                                T(lambda e, hh=hh, wT=wT, nh=nh, ech=ech,
                                  par=par: e.matmul(
                                    pp[0][:, 0:512],
                                    wT[:, ech * HPC * 128 + hh * 128:
                                       ech * HPC * 128 + (hh + 1) * 128],
                                    hT[par][:, ech * TQ + nh * 512:
                                            ech * TQ + (nh + 1) * 512],
                                    start=(ech == 0), stop=(ech == KC - 1)))
                            W("v", ct)
                            V(lambda e, hh=hh, qk=qk: e.tensor_scalar_add(
                                rt1[:, :], pp[0][:, 0:512],
                                bqk[:, qk * HPC + hh: qk * HPC + hh + 1]))
                            W("a", cv)
                            W("a", ct)
                            A(lambda e: e.activation(rt1b[:, :], rt1[:, :],
                                                     AF.Copy))
                            W("t", ca)
                            T(lambda e: e.matmul(pp[1][:, 0:512], rp[:, :],
                                                 rt1b[:, :],
                                                 start=True, stop=True))
                            W("v", ct)
                            rec.wait("v", ch("cb_ws"))
                            V(lambda e, tpos=tpos: e.tensor_mul(
                                rt2[:, :], rt1[:, :],
                                cosb[:, tpos:tpos + 512]))
                            W("v", cv)
                            V(lambda e, tpos=tpos: e.tensor_mul(
                                rt3[:, :], pp[1][:, 0:512],
                                sinb[:, tpos:tpos + 512]))
                            W("v", cv)
                            V(lambda e, hh=hh, dstT=dstT, tpos=tpos:
                              e.tensor_add(
                                dstT[:, hh * S_ + tpos: hh * S_ + tpos + 512],
                                rt2[:, :], rt3[:, :]))
                for ts in range(TQ // 128):
                    W("t", cv)
                    W("t", ca)
                    for ech in range(KC):
                        T(lambda e, ts=ts, ech=ech, par=par: e.matmul(
                            pp[2][:, 0:HPC * 128],
                            hT[par][:, ech * TQ + ts * 128:
                                    ech * TQ + (ts + 1) * 128],
                            wvT[:, ech * HPC * 128:(ech + 1) * HPC * 128],
                            start=(ech == 0), stop=(ech == KC - 1)))
                    kt = i * (TQ // 128) + ts
                    W("v", ct)
                    for hh in range(HPC):
                        V(lambda e, hh=hh, kt=kt: e.tensor_add(
                            vb[:, (hh * NKT + kt) * 132:
                               (hh * NKT + kt) * 132 + 128],
                            pp[2][:, hh * 128:(hh + 1) * 128],
                            bvb[:, hh * 128:(hh + 1) * 128]))
                if i + 1 < NT:
                    rec.wait("s", ct)
                    load_hT(b, i + 1, (i + 1) % 2)
                elif b + 1 < c.B:
                    rec.wait("s", ct)
                    load_hT(b + 1, 0, 0)
            for qt in range(S_ // 512):
                for hh in range(HPC):
                    nk = 4 * qt + 4
                    W("t", cv)
                    for kt in range(nk):
                        pbuf = kt % 2
                        ps = pp[3 + pbuf]
                        W("t", ca)
                        T(lambda e, hh=hh, qt=qt, kt=kt, ps=ps: e.matmul(
                            ps[:, 0:512],
                            kT[:, hh * S_ + kt * 128: hh * S_ + (kt + 1) * 128],
                            qT[:, hh * S_ + qt * 512: hh * S_ + (qt + 1) * 512],
                            start=True, stop=True))
                        diag = kt - 4 * qt
                        if diag >= 0:
                            W("v", ct)
                            V(lambda e, ps=ps, diag=diag: e.tensor_add(
                                ps[:, 0:512], ps[:, 0:512],
                                dmask[:, diag * 512:(diag + 1) * 512]))
                            W("a", cv)
                        else:
                            W("a", ct)
                        W("a", cv)
                        W("a", ch("cb_pv"))
                        A(lambda e, ps=ps, pbuf=pbuf, b=b_sh: e.activation(
                            Pb[pbuf][:, :], ps[:, 0:512], AF.Exp,
                            scale=float(128 ** -0.5), bias=b))
                        W("t", ca)
                        for sq in range(4):
                            rec.op("t", lambda e, pbuf=pbuf, sq=sq, hh=hh,
                                   kt=kt, nk=nk: e.matmul(
                                po[sq][:, 0:132],
                                Pb[pbuf][:, sq * 128:(sq + 1) * 128],
                                vb[:, (hh * NKT + kt) * 132:
                                   (hh * NKT + kt + 1) * 132],
                                start=(kt == 0), stop=(kt == nk - 1)),
                                tick=ch("cb_pv"))
                    W("v", ch("cb_pv"))
                    W("v", cso)
                    for sq in range(4):
                        V(lambda e, sq=sq: e.reciprocal(
                            ocol[:, sq:sq + 1], po[sq][:, 128:129]))
                        W("v", cv)
                        V(lambda e, sq=sq, hh=hh: e.tensor_scalar_mul(
                            onb[:, (sq * HPC + hh) * 128:
                                (sq * HPC + hh + 1) * 128],
                            po[sq][:, 0:128], ocol[:, sq:sq + 1]))
                        W("v", cv)
                W("s", cv)
                for sq in range(4):
                    r0 = b * S_ + qt * 512 + sq * 128
                    rec.op("s", lambda e, sq=sq, r0=r0: e.dma_start(
                        out=G.agin_nb[r0:r0 + 128, :],
                        in_=onb[:, sq * HPC * 128:(sq + 1) * HPC * 128]),
                        dma=cso)
        W("g", cso)
    rec.op("g", lambda e: e.collective_compute(
        "AllGather", ALU.bypass, replica_groups=[list(range(c.NC))],
        ins=[G.agin_nb[:, :]], outs=[G.ag_nb[:, :]]), tick=ch("cc_nb"))
    G.dbg("nb", G.ag_nb, after_ch=ch("cc_nb"))





def host_consts(c):
    i128 = np.arange(128)
    pos = np.arange(c.S, dtype=np.float64)[:, None]
    theta = np.repeat(-np.arange(0, 128, 2, dtype=np.float64) / 128.0, 2)
    theta = np.power(10000.0, theta)
    ang = pos * theta[None, :]
    perm = np.zeros((128, 128), dtype=np.float32)
    for i in range(0, 128, 2):
        perm[i, i + 1] = -1.0
        perm[i + 1, i] = 1.0
    return {
        "identf": np.eye(128, dtype=np.float32),
        "triu1": np.triu(np.ones((128, 128), dtype=np.float32)),
        "e127": np.broadcast_to((i128[:, None] == 127), (128, 128))
                .astype(np.float32).copy(),
        "cosb": np.ascontiguousarray(np.cos(ang).T.astype(np.float32)),
        "sinb": np.ascontiguousarray(np.sin(ang).T.astype(np.float32)),
        "ropeperm": perm.T.copy(),
    }


def shard_inputs(c, inp, core):
    hp = c.HPC
    h0 = core * hp
    f32 = lambda k: np.asarray(inp[k], dtype=np.float32)
    x = f32("L0_attn_a_hidden_states").reshape(c.T, c.D)
    conv = f32("L0_attn_a_conv_weight")
    convrows = []
    for hh in range(hp):
        h = h0 + hh
        for part in range(3):
            convrows.append(conv[part * c.D + 128 * h: part * c.D + 128 * (h + 1)])
    C = np.ascontiguousarray
    m = {
        "x0": C(x[core * c.Tc:(core + 1) * c.Tc]),
        "wqkvz": C(f32("L0_attn_a_W_qkvz")[:, 512 * h0: 512 * (h0 + hp)]),
        "wba": C(f32("L0_attn_a_W_ba")[:, 2 * h0: 2 * (h0 + hp)]),
        "convw": C(np.concatenate(convrows, axis=0)),
        "dtb_alog": C(np.stack([f32("L0_attn_a_dt_bias")[h0:h0 + hp],
                                f32("L0_attn_a_A_log")[h0:h0 + hp]])),
        "normw": C(f32("L0_attn_a_norm_weight")[None, :]),
        "wout": C(f32("L0_attn_a_W_out")),
        "wq": C(f32("L0_attn_b_Wq")[128 * h0:128 * (h0 + hp)]),
        "wk": C(f32("L0_attn_b_Wk")[128 * h0:128 * (h0 + hp)]),
        "wv": C(f32("L0_attn_b_Wv")[128 * h0:128 * (h0 + hp)]),
        "bqkv": C(np.stack([f32("L0_attn_b_bq")[128 * h0:128 * (h0 + hp)],
                            f32("L0_attn_b_bk")[128 * h0:128 * (h0 + hp)],
                            f32("L0_attn_b_bv")[128 * h0:128 * (h0 + hp)]])),
        "wo": C(f32("L0_attn_b_Wo")),
        "bo": C(f32("L0_attn_b_bo")[None, :]),
        "routwT": C(f32("L0_ffn_router_w").T),
        "w1": C(f32("L0_ffn_w1")),
        "v1": C(f32("L0_ffn_v1")),
        "w2": C(f32("L0_ffn_w2")),
        "sg": C(f32("L0_ffn_sg_w")),
        "su": C(f32("L0_ffn_su_w")),
        "sd": C(f32("L0_ffn_sd_w")),
    }
    m.update(host_consts(c))
    return m


def kernel(**inputs):
    c = CFG()
    G = build_nc(c)
    in_maps = [shard_inputs(c, inputs, core) for core in range(c.NC)]
    res = run_bass_kernel_spmd(G.nc, in_maps, core_ids=list(range(c.NC)))
    outs = [np.asarray(res.results[i]["out"]) for i in range(c.NC)]
    return np.concatenate(outs, axis=0).reshape(c.B, c.S, c.D)


# version 14
# speedup vs baseline: 1.0079x; 1.0079x over previous
import math
from contextlib import ExitStack
from types import SimpleNamespace

import numpy as np
import ml_dtypes

import concourse.bass as bass
import concourse.mybir as mybir
from concourse.bass_utils import run_bass_kernel_spmd

F32 = mybir.dt.float32
BF16 = mybir.dt.bfloat16
AF = mybir.ActivationFunctionType
ALU = mybir.AluOpType
AX = mybir.AxisListType
NEG = -1.0e30


class CFG:

    def __init__(self, **kw):
        self.B = 4
        self.S = 4096
        self.HK = 16
        self.E = 16
        self.F = 1024
        self.FS = 2048
        self.NC = 8
        self.TQ = 512
        self.FFN_TP = 512
        self.__dict__.update(kw)
        assert self.S % 128 == 0 and self.HK % self.NC == 0
        self.D = self.HK * 128
        self.T = self.B * self.S
        self.Tc = self.T // self.NC
        self.HPC = self.HK // self.NC
        self.KC = self.D // 128
        self.NCH = self.S // 128
        assert self.Tc % 128 == 0
        assert self.TQ % 128 == 0 and self.S % self.TQ == 0
        assert self.FFN_TP % 128 == 0 and self.Tc % self.FFN_TP == 0
        assert self.F % 128 == 0 and self.FS % 512 == 0


class Ch:

    def __init__(self, nc, name):
        self.sem = nc.alloc_semaphore(name=name)
        self.n = 0
        self.name = name


class Rec:

    ENG = ("t", "v", "a", "g", "s")

    def __init__(self, nc):
        self.nc = nc
        self.limit = None
        self.count = 0
        self.items = {e: [] for e in self.ENG}
        self.last_op = {e: None for e in self.ENG}
        self.dma_out = {e: {} for e in self.ENG}
        self.pending_bar = {e: None for e in self.ENG}
        self.waited = {e: {} for e in self.ENG}

    def _arm(self, eng):
        if self.pending_bar[eng] is not None:
            ch, val = self.pending_bar[eng]
            self.pending_bar[eng] = None
            self._wait(eng, ch, val)

    def _wait(self, eng, ch, val):
        if self.waited[eng].get(ch, -1) >= val:
            return
        self.waited[eng][ch] = val
        self.items[eng].append(("wait", ch, val))

    def op(self, eng, fn, tick=None, dma=None):
        self.count += 1
        if self.limit is not None and self.count > self.limit:
            return ["op", lambda e: None, []]
        self._arm(eng)
        ticks = []
        if tick is not None:
            chx, k = tick if isinstance(tick, tuple) else (tick, 1)
            ticks.append((chx, k))
            chx.n += k
        if dma is not None:
            ticks.append((dma, 16))
            dma.n += 16
            self.dma_out[eng][dma] = dma.n
        assert len(ticks) <= 1, "max 1 sem update per instruction"
        item = ["op", fn, ticks]
        self.items[eng].append(item)
        self.last_op[eng] = item
        return item

    def wait(self, eng, ch, val=None):
        if self.limit is not None and self.count > self.limit:
            return
        self._arm(eng)
        self._wait(eng, ch, ch.n if val is None else val)

    def bar(self, bar_ch):
        for e in self.ENG:
            if self.last_op[e] is None:
                continue
            outs = self.dma_out[e]
            if outs:
                self._arm(e)
                for chx, val in list(outs.items()):
                    self._wait(e, chx, val)
                outs.clear()
                self.op(e, lambda en: en.nop(), tick=bar_ch)
            elif len(self.last_op[e][2]) < 1:
                self.last_op[e][2].append((bar_ch, 1))
                bar_ch.n += 1
            else:
                self.op(e, lambda en: en.nop(), tick=bar_ch)
            self.last_op[e] = None
        tgt = bar_ch.n
        for e in self.ENG:
            self.pending_bar[e] = (bar_ch, tgt)

    def replay(self, block):
        def runner(items):
            def go(eng):
                for it in items:
                    if it[0] == "op":
                        inst = it[1](eng)
                        for chx, k in it[2]:
                            inst.then_inc(chx.sem, k)
                    else:
                        eng.wait_ge(it[1].sem, it[2])
            return go

        block.tensor(runner(self.items["t"]))
        block.vector(runner(self.items["v"]))
        block.scalar(runner(self.items["a"]))
        block.gpsimd(runner(self.items["g"]))
        block.sync(runner(self.items["s"]))




def build_nc(cfg, debug=(), upto=99, rec_limit=None, skip=()):
    c = cfg
    nc = bass.Bass("TRN2", target_bir_lowering=False)
    G = SimpleNamespace(c=c, nc=nc, debug=set(debug), dbg_out={}, upto=upto,
                        skip=set(skip), in_names=[])
    G._rec_limit = rec_limit

    def ein(name, shape, dtype=F32):
        ap = nc.dram_tensor(name, list(shape), dtype, kind="ExternalInput").ap()
        setattr(G, name, ap)
        G.in_names.append(name)
        return ap

    def idram(name, shape, dtype):
        ap = nc.dram_tensor(name, list(shape), dtype, kind="Internal").ap()
        setattr(G, name, ap)
        return ap

    def idram2(name, shape, dtype, space):
        ap = nc.dram_tensor(name, list(shape), dtype, kind="Internal",
                            addr_space=space).ap()
        setattr(G, name, ap)
        return ap

    ein("x0", (c.Tc, c.D))
    if "in_w" not in G.skip:
        ein("wqkvz", (c.D, c.HPC * 512))
        ein("wba", (c.D, c.HPC * 2))
        ein("convw", (3 * c.HPC * 128, 4))
        ein("dtb_alog", (2, c.HPC))
        ein("normw", (1, 128))
        ein("wout", (c.D, c.D))
        ein("wq", (c.HPC * 128, c.D))
        ein("wk", (c.HPC * 128, c.D))
        ein("wv", (c.HPC * 128, c.D))
        ein("bqkv", (3, c.HPC * 128))
        ein("wo", (c.D, c.D))
        ein("bo", (1, c.D))
        ein("routwT", (c.D, c.E))
        ein("w1", (c.E, c.D, c.F))
        ein("v1", (c.E, c.D, c.F))
        ein("w2", (c.E, c.F, c.D))
        ein("sg", (c.FS, c.D))
        ein("su", (c.FS, c.D))
        ein("sd", (c.D, c.FS))
    if "in_c" not in G.skip:
        if "c1" not in G.skip:
            ein("identf", (128, 128))
        if "c2" not in G.skip:
            ein("triu1", (128, 128))
        if "c3" not in G.skip:
            ein("e127", (128, 128))
        if "c4" not in G.skip:
            ein("cosb", (128, c.S))
        if "c5" not in G.skip:
            ein("sinb", (128, c.S))
        if "c6" not in G.skip:
            ein("ropeperm", (128, 128))

    G.out = nc.dram_tensor("out", [c.Tc, c.D], F32, kind="ExternalOutput").ap()
    if "gates" in G.debug:
        G.dbg_gates = nc.dram_tensor("dbg_gates", [c.Tc, c.E], F32,
                                     kind="ExternalOutput").ap()
    if "scores" in G.debug:
        G.dbg_scores = nc.dram_tensor("dbg_scores", [c.Tc, c.E], F32,
                                      kind="ExternalOutput").ap()

    if "int_ag" not in G.skip:
        idram("ag_h1_in", (c.Tc, c.D), BF16)
        idram2("ag_h1", (c.T, c.D), BF16, "Shared")
        idram("agin_na", (c.T, c.HPC * 128), BF16)
        idram2("ag_na", (c.NC * c.T, c.HPC * 128), BF16, "Shared")
        idram("my_na", (c.Tc, c.D), BF16)
        idram("ag_h3_in", (c.Tc, c.D), BF16)
        idram2("ag_h3", (c.T, c.D), BF16, "Shared")
        idram("agin_nb", (c.T, c.HPC * 128), BF16)
        idram2("ag_nb", (c.NC * c.T, c.HPC * 128), BF16, "Shared")
        idram("my_nb", (c.Tc, c.D), BF16)
    if "int_x" not in G.skip:
        idram("x_cur", (c.Tc, c.D), F32)
        idram("x_cur2", (c.Tc, c.D), F32)
        idram("h_loc", (c.Tc, c.D), BF16)
        idram("rowscr", (c.B * c.HPC * 2 * (c.S // 128), 128), F32)
    if "int_w" not in G.skip:
        idram("wq_b", (c.HPC * 128, c.D), BF16)
        idram("wk_b", (c.HPC * 128, c.D), BF16)
        idram("wv_b", (c.HPC * 128, c.D), BF16)
        idram("wo_b", (c.D, c.D), BF16)
        idram("w1_b", (c.E, c.D, c.F), BF16)
        idram("v1_b", (c.E, c.D, c.F), BF16)
        idram("w2_b", (c.E, c.F, c.D), BF16)
        idram("sg_b", (c.FS, c.D), BF16)
        idram("su_b", (c.FS, c.D), BF16)
        idram("sd_b", (c.D, c.FS), BF16)
        idram("routwT_b", (c.D, c.E), BF16)

    rec = G.rec = Rec(nc)
    rec.limit = G._rec_limit
    chans = {}

    def ch(name):
        if name not in chans:
            chans[name] = Ch(nc, name)
        return chans[name]

    G.ch = ch

    def dbg(tag, src_ap, after_ch=None):
        if tag not in G.debug:
            return
        o = nc.dram_tensor("dbg_" + tag, list(src_ap.shape),
                           src_ap.dtype, kind="ExternalOutput").ap()
        G.dbg_out[tag] = o
        cd = ch("c_dbg")
        if after_ch is not None:
            rec.wait("s", after_ch)
        rec.op("s", lambda e: e.dma_start(out=o, in_=src_ap), dma=cd)
        rec.wait("s", cd)

    G.dbg = dbg

    with ExitStack() as ST:
        G.ST = ST

        def sb(name, shape, dtype):
            h = ST.enter_context(nc.sbuf_tensor(name, list(shape), dtype))
            setattr(G, "s_" + name, h)
            return h

        G.sb = sb
        if "sbufp" in G.skip:
            G.upto = -9
        _csts = {}
        _cst_arena = ST.enter_context(nc.sbuf_tensor("cst_arena", [128, 64], F32))

        def cst(val):
            if val not in _csts:
                i = len(_csts)
                assert i < 64
                rec.op("v", lambda e, i=i, v=val: e.memset(
                    _cst_arena[:, i:i + 1], v), tick=ch("c_cst"))
                _csts[val] = i
            for e in Rec.ENG:
                if e != "v":
                    rec.wait(e, ch("c_cst"))
            return _cst_arena[:, _csts[val]:_csts[val] + 1]

        G.cst = cst
        if "sbufp" not in G.skip:
            sb("identb", (128, 128), BF16)
            sb("identf_s", (128, 128), F32)
            sb("ones_col_b", (128, 1), BF16)
            sb("ones_row_f", (1, 512), F32)
            sb("ones_row_b", (1, 512), BF16)
            sb("one11_f", (1, 1), F32)
            sb("one11_b", (1, 1), BF16)
            sb("wnorm_bc", (128, 128), F32)
            sb("mask_ui", (128, 128), F32)
            sb("mask_us", (128, 128), F32)
            sb("triu1_s", (128, 128), F32)
            sb("e127_s", (128, 128), F32)

        if G.upto <= -9:
            tmin = G.sb("tmin", (128, c.D), F32)
            cmin = ch("cmin")
            rec.op("s", lambda e: e.dma_start(out=tmin[:, :],
                                              in_=G.x0[0:128, :]), dma=cmin)
            rec.wait("s", cmin)
            rec.op("s", lambda e: e.dma_start(out=G.out[0:128, :],
                                              in_=tmin[:, :]), dma=cmin)
            rec.wait("s", cmin)
            with nc.Block() as block:
                rec.replay(block)
            return G
        p_prologue(G)
        if G.upto >= -1:
            p_ln(G, src=G.x0, dst=G.ag_h1_in, tag="ln1", resid_src=None)
        if G.upto >= 0:
            rec.op("g", lambda e: e.collective_compute(
                "AllGather", ALU.bypass, replica_groups=[list(range(c.NC))],
                ins=[G.ag_h1_in[:, :]], outs=[G.ag_h1[:, :]]), tick=ch("cc_h1"))
            dbg("h1", G.ag_h1, after_ch=ch("cc_h1"))

        rec.bar(ch("bar"))
        if G.upto >= 1:
            p_rest(G)

        rec.bar(ch("bar"))
        rec.wait("s", ch("bar"))

        with nc.Block() as block:
            rec.replay(block)

    return G


def p_prologue(G):
    c, nc, rec, ch = G.c, G.nc, G.rec, G.ch
    cl = ch("c_const")

    def dma(eng, out_ap, in_ap, chan=None):
        rec.op(eng, lambda e, o=out_ap, i=in_ap: e.dma_start(out=o, in_=i),
               dma=chan or cl)

    dma("s", G.s_identf_s[:, :], G.identf[:, :])
    dma("s", G.s_triu1_s[:, :], G.triu1[:, :])
    dma("s", G.s_e127_s[:, :], G.e127[:, :])
    dma("g", G.s_identb[:, :], G.identf[:, :], chan=ch("c_const_g"))
    rec.op("v", lambda e: e.memset(G.s_ones_col_b[:, :], 1.0))
    rec.op("v", lambda e: e.memset(G.s_ones_row_f[:, :], 1.0))
    rec.op("v", lambda e: e.memset(G.s_ones_row_b[:, :], 1.0))
    rec.op("v", lambda e: e.memset(G.s_one11_f[:, :], 1.0))
    rec.op("v", lambda e: e.memset(G.s_one11_b[:, :], 1.0), tick=ch("c_v"))
    gs = ch("c_g")
    rec.op("g", lambda e: e.memset(G.s_mask_ui[:, :], 0.0), tick=gs)
    rec.op("g", lambda e: e.memset(G.s_mask_us[:, :], 0.0), tick=gs)
    rec.wait("g", gs)
    rec.op("g", lambda e: e.affine_select(
        G.s_mask_ui[:, :], G.s_mask_ui[:, :], [[1, 128]], ALU.is_ge, NEG,
        base=0, channel_multiplier=-1), tick=gs)
    rec.op("g", lambda e: e.affine_select(
        G.s_mask_us[:, :], G.s_mask_us[:, :], [[1, 128]], ALU.is_gt, NEG,
        base=0, channel_multiplier=-1), tick=gs)
    nw = G.sb("nw_row", (1, 128), F32)
    dma("s", nw[:, :], G.normw[:, :])
    with nc.psum_tensor("ps_c0", [128, 128], F32) as ps:
        rec.wait("t", cl)
        rec.wait("t", ch("c_const_g"))
        rec.wait("t", ch("c_v"))
        rec.op("t", lambda e: e.matmul(ps[:, :], G.s_ones_row_f[:1, :128],
                                       nw[:, :], start=True, stop=True),
               tick=ch("c_t"))
        rec.wait("v", ch("c_t"))
        rec.op("v", lambda e: e.tensor_copy(G.s_wnorm_bc[:, :], ps[:, :]),
               tick=ch("c_v"))


def p_ln(G, src, dst, tag, resid_src=None, resid_ps=None, x_dst=None):
    c, nc, rec, ch = G.c, G.nc, G.rec, G.ch
    nt = c.Tc // 128
    with ExitStack() as S:
        xa = S.enter_context(nc.sbuf_tensor(tag + "_xa", [128, c.D], F32))
        xb = S.enter_context(nc.sbuf_tensor(tag + "_xb", [128, c.D], F32))
        sq = S.enter_context(nc.sbuf_tensor(tag + "_sq", [128, c.D], F32))
        hb = S.enter_context(nc.sbuf_tensor(tag + "_h", [128, c.D], BF16))
        cols = S.enter_context(nc.sbuf_tensor(tag + "_c", [128, 8], F32))
        cld, cv, ca, cst = (ch(tag + k) for k in ("_ld", "_v", "_a", "_st"))
        vdone = [0, 0]
        adone = [0, 0]
        for i in range(nt):
            sl = i % 2
            xt = (xa, xb)[sl]
            co = 4 * sl
            rec.wait("s", cv, vdone[sl])
            rec.wait("s", ca, adone[sl])
            rec.op("s", lambda e, o=xt, t0=i * 128: e.dma_start(
                out=o[:, :], in_=src[t0:t0 + 128, :]), dma=cld)
            rec.wait("v", cld)
            rec.wait("v", ca, adone[sl])
            rec.op("v", lambda e, x=xt, co=co: e.reduce_sum(
                cols[:, co:co + 1], x[:, :], AX.X), tick=cv)
            rec.wait("a", cv)
            rec.op("a", lambda e, co=co: e.activation(
                cols[:, co + 1:co + 2], cols[:, co:co + 1], AF.Copy,
                scale=1.0 / c.D), tick=ca)
            rec.wait("v", ca)
            rec.op("v", lambda e, x=xt, co=co: e.tensor_scalar_sub(
                x[:, :], x[:, :], cols[:, co + 1:co + 2]), tick=cv)
            rec.wait("a", cv)
            rec.op("a", lambda e, x=xt: e.activation(
                sq[:, :], x[:, :], AF.Square), tick=ca)
            rec.wait("v", ca)
            rec.op("v", lambda e, co=co: e.reduce_sum(
                cols[:, co + 2:co + 3], sq[:, :], AX.X), tick=cv)
            eps_ap = G.cst(1e-5)
            rec.wait("a", cv)
            rec.op("a", lambda e, co=co, b=eps_ap: e.activation(
                cols[:, co + 3:co + 4], cols[:, co + 2:co + 3], AF.Sqrt,
                scale=1.0 / c.D, bias=b), tick=ca)
            adone[sl] = ca.n
            rec.wait("v", ca)
            rec.op("v", lambda e, co=co: e.reciprocal(
                cols[:, co + 2:co + 3], cols[:, co + 3:co + 4]), tick=cv)
            rec.wait("v", cv)
            rec.wait("v", cst)
            rec.op("v", lambda e, x=xt, co=co: e.tensor_scalar_mul(
                hb[:, :], x[:, :], cols[:, co + 2:co + 3]), tick=cv)
            vdone[sl] = cv.n
            rec.wait("s", cv)
            rec.op("s", lambda e, t0=i * 128: e.dma_start(
                out=dst[t0:t0 + 128, :], in_=hb[:, :]), dma=cst)
        rec.wait("s", cst)
        rec.wait("g", cst)


def p_rest(G):
    c, ch = G.c, G.ch
    rec = G.rec
    if G.upto < 1:
        return
    p_attn_a(G)
    rec.bar(ch("bar"))
    if G.upto < 2:
        return
    p_mixer_out(G, tag="mo1", ag=G.ag_na, my=G.my_na, cc="cc_na",
                w_direct=G.wout, bias=None, x_src=G.x0, x_dst=G.x_cur,
                h_dst=G.h_loc, dep=None)
    G.dbg("x1", G.x_cur, after_ch=ch("mo1_st"))
    G.dbg("h2", G.h_loc, after_ch=ch("mo1_st"))
    rec.bar(ch("bar"))
    if G.upto < 3:
        return
    p_ffn(G, tag="f1", h_src=G.h_loc, x_src=G.x_cur, x_dst=G.x_cur2,
          h_dst=G.ag_h3_in, last=False, dep=ch("mo1_st"))
    G.dbg("x2", G.x_cur2, after_ch=ch("f1_st"))
    G.rec.op("g", lambda e: e.collective_compute(
        "AllGather", ALU.bypass,
        replica_groups=[list(range(c.NC))],
        ins=[G.ag_h3_in[:, :]], outs=[G.ag_h3[:, :]]), tick=ch("cc_h3"))
    G.dbg("h3", G.ag_h3, after_ch=ch("cc_h3"))
    rec.bar(ch("bar"))
    if G.upto < 4:
        return
    p_attn_b(G)
    rec.bar(ch("bar"))
    if G.upto < 5:
        return
    p_mixer_out(G, tag="mo2", ag=G.ag_nb, my=G.my_nb, cc="cc_nb",
                w_direct=None, bias=G.bo, x_src=G.x_cur2, x_dst=G.x_cur,
                h_dst=G.h_loc, dep=ch("f1_st"))
    rec.bar(ch("bar"))
    if G.upto < 6:
        return
    p_ffn(G, tag="f2", h_src=G.h_loc, x_src=G.x_cur, x_dst=G.out,
          h_dst=None, last=True, dep=ch("mo2_st"))


def p_attn_a(G):
    c, nc, rec, ch = G.c, G.nc, G.rec, G.ch
    HPC, TQ, KC = c.HPC, c.TQ, c.KC
    NT = c.S // TQ
    NSUB = TQ // 128
    FQ = 3 * HPC
    ZW = HPC * 128 + 2 * HPC
    MS = 13
    NLV = 7
    with ExitStack() as S:
        def sb(name, shape, dtype):
            return S.enter_context(nc.sbuf_tensor("aa_" + name, list(shape), dtype))

        hT = [sb("hT0", (128, KC * TQ), BF16), sb("hT1", (128, KC * TQ), BF16)]
        wqkv = sb("wqkv", (128, KC * FQ * 128), BF16)
        wzba = sb("wzba", (128, KC * ZW), BF16)
        convd = sb("convd", (128, FQ * 4 * 128), BF16)
        convc = sb("convc", (128, FQ * 4), F32)
        qkvraw = sb("qkvraw", (128, FQ * (TQ + 3)), BF16)
        qkvs = [sb("qkvs0", (128, FQ * TQ), BF16), sb("qkvs1", (128, FQ * TQ), BF16)]
        ktm = [sb("ktm0", (128, HPC * TQ), BF16), sb("ktm1", (128, HPC * TQ), BF16)]
        vtm = [sb("vtm0", (128, HPC * TQ), BF16), sb("vtm1", (128, HPC * TQ), BF16)]
        zsil = [sb("zs0", (128, NSUB * HPC * 128), BF16),
                sb("zs1", (128, NSUB * HPC * 128), BF16)]
        mats = [sb("mats0", (128, HPC * MS * NSUB), F32),
                sb("mats1", (128, HPC * MS * NSUB), F32)]
        rowb = sb("rowb", (1, HPC * 3 * TQ), F32)
        rowt = sb("rowt", (1, 2 * 512), F32)
        rowft = sb("rowft", (1, HPC * TQ), F32)
        rowbt = sb("rowbt", (1, HPC * TQ), F32)
        rowbb = sb("rowbb", (1, HPC * 3 * TQ), BF16)
        rowsq = sb("rowsq", (128, TQ), BF16)
        rkcol = sb("rkcol", (128, HPC * (TQ // 128)), F32)
        rawAU = sb("rawAU", (128, 4 * 256), F32)
        rbt2 = [sb("rbtA", (16, 128), F32), sb("rbtB", (16, 128), F32)]
        headc = sb("headc", (128, 2 * HPC), F32)
        dtb_r = sb("dtb_r", (1, 2 * HPC), F32)
        Mb = sb("Mb", (128, 4 * 128), F32)
        Eb = sb("Eb", (128, 4 * 2 * 128), F32)
        mUb = sb("mUb", (128, 4 * 128), F32)
        mab = sb("mab", (128, 4 * 128), F32)
        Upow = sb("Upow", (128, 4 * NLV * 128), BF16)
        Tpow = sb("Tpow", (128, 4 * 2 * 128), BF16)
        Xb = sb("Xb", (128, 4 * 2 * 256), BF16)
        NSL = 2
        attT = sb("attT", (128, NSL * HPC * 128), BF16)
        wnT = sb("wnT", (128, NSL * HPC * 128), BF16)
        kdec = sb("kdec", (128, NSL * HPC * 128), BF16)
        qgT = sb("qgT", (128, NSL * HPC * 128), BF16)
        Xfin = sb("Xfin", (128, NSL * HPC * 256), BF16)
        vnew = sb("vnew", (128, HPC * 128), BF16)
        colsc = sb("colsc", (128, 4 * 8), F32)
        oc2 = sb("oc2", (128, HPC * 4), F32)
        Sf = sb("Sf", (128, HPC * 128), F32)
        Sb = sb("Sb", (128, HPC * 128), BF16)
        onorm = sb("onorm", (128, HPC * 128), BF16)
        ot1 = sb("ot1", (128, HPC * 128), F32)
        osq = sb("osq", (128, HPC * 128), F32)

        pb = [S.enter_context(nc.psum_tensor(f"aa_pb{i}", [128, 512], F32))
              for i in range(8)]
        DB = [pb[5], pb[1], pb[6], pb[2]]
        DBtr = [d[:, 320:384].bitcast(BF16) for d in DB]

        cw = ch("ca_w")
        cld2 = [ch("ca_ld0"), ch("ca_ld1")]
        ct = ch("ca_t")
        cv = ch("ca_v")
        ca = ch("ca_a")
        cg = ch("ca_g")
        cso = ch("ca_so")
        crw = ch("ca_rw")

        def T(fn):
            rec.op("t", fn, tick=ct)

        def V(fn):
            rec.op("v", fn, tick=cv)

        def A(fn):
            rec.op("a", fn, tick=ca)

        def P(fn):
            rec.op("g", fn, tick=cg)

        W = rec.wait
        res = {}

        def use(eng, *names):
            for n in names:
                for chx, val in res.get(n, ()):
                    rec.wait(eng, chx, val)

        def setr(name, *chans):
            res[name] = [(chx, chx.n) for chx in chans]

        for hh in range(HPC):
            rec.op("g", lambda e, hh=hh: e.dma_start(
                out=wqkv[:, :].rearrange("p (ch f) -> p ch f", f=FQ * 128)
                    [:, :, hh * 384:(hh + 1) * 384],
                in_=G.wqkvz.rearrange("(ch p) f -> p ch f", p=128)
                    [:, :, 512 * hh:512 * hh + 384]), dma=cw)
            rec.op("g", lambda e, hh=hh: e.dma_start(
                out=wzba[:, :].rearrange("p (ch f) -> p ch f", f=ZW)
                    [:, :, hh * 128:(hh + 1) * 128],
                in_=G.wqkvz.rearrange("(ch p) f -> p ch f", p=128)
                    [:, :, 512 * hh + 384:512 * hh + 512]), dma=cw)
        rec.op("g", lambda e: e.dma_start(
            out=wzba[:, :].rearrange("p (ch f) -> p ch f", f=ZW)
                [:, :, HPC * 128:],
            in_=G.wba.rearrange("(ch p) f -> p ch f", p=128)), dma=cw)
        rec.op("g", lambda e: e.dma_start(
            out=convc[:, :].rearrange("p (fc j) -> p fc j", j=4),
            in_=G.convw.rearrange("(fc p) j -> p fc j", p=128)), dma=cw)
        rec.wait("g", cw)
        for fc in range(FQ):
            for j in range(4):
                off = (fc * 4 + j) * 128
                P(lambda e, o=off: e.memset(convd[:, o:o + 128], 0.0))
                W("g", cg)
                P(lambda e, o=off, cc=fc * 4 + j: e.tensor_scalar_add(
                    convd[:, o:o + 128], convd[:, o:o + 128],
                    convc[:, cc:cc + 1]))
                W("g", cg)
                P(lambda e, o=off: e.affine_select(
                    convd[:, o:o + 128], convd[:, o:o + 128],
                    [[1, 128]], ALU.is_equal, 0.0, base=0,
                    channel_multiplier=-1))
        csld = ch("ca_sld")
        rec.op("s", lambda e: e.dma_start(
            out=dtb_r[:, 0:HPC], in_=G.dtb_alog[0:1, :]), dma=csld)
        rec.op("s", lambda e: e.dma_start(
            out=dtb_r[:, HPC:2 * HPC], in_=G.dtb_alog[1:2, :]), dma=csld)
        W("a", csld)
        A(lambda e: e.activation(dtb_r[:, HPC:2 * HPC], dtb_r[:, HPC:2 * HPC],
                                 AF.Exp))
        W("a", ca)
        A(lambda e: e.activation(dtb_r[:, HPC:2 * HPC], dtb_r[:, HPC:2 * HPC],
                                 AF.Copy, scale=-1.0))
        W("t", ca)
        T(lambda e: e.matmul(pb[3][:, 0:2 * HPC], G.s_ones_row_f[:1, :128],
                             dtb_r[:, :], start=True, stop=True))
        W("v", ct)
        V(lambda e: e.tensor_copy(headc[:, :], pb[3][:, 0:2 * HPC]))
        setr("pb3", cv)

        cwp = ch("c_wprep")

        def emit_casts():
            for s_, d_ in ((G.routwT, G.routwT_b), (G.sg, G.sg_b),
                           (G.su, G.su_b), (G.sd, G.sd_b)):
                rec.op("g", lambda e, o=d_, i_=s_: e.dma_start(
                    out=o[:, :], in_=i_[:, :]), dma=cwp)
            for s_, d_ in ((G.w1, G.w1_b), (G.v1, G.v1_b), (G.w2, G.w2_b)):
                for e_i in range(c.E):
                    rec.op("g", lambda e, o=d_[e_i], i_=s_[e_i]: e.dma_start(
                        out=o, in_=i_), dma=cwp)
            for s_, d_ in ((G.wq, G.wq_b), (G.wk, G.wk_b), (G.wv, G.wv_b),
                           (G.wo, G.wo_b)):
                rec.op("g", lambda e, o=d_, i_=s_: e.dma_start(
                    out=o[:, :], in_=i_[:, :]), dma=cwp)

        W("s", ch("cc_h1"))

        def load_hT(b, i, par):
            for chk in range(KC):
                rec.op("s", lambda e, b=b, i=i, par=par, chk=chk: e.dma_start(
                    out=hT[par][:, chk * TQ:(chk + 1) * TQ],
                    in_=G.ag_h1[b * c.S + i * TQ: b * c.S + (i + 1) * TQ,
                                chk * 128:(chk + 1) * 128],
                    transpose=True), dma=cld2[par])

        def stage_a_gen(b, i, par, first=False):
            W("t", cld2[par])
            W("t", cw)
            use("t", f"tdone{par}")
            W("a", cg)
            if first:
                W("g", ca)
                for fc in range(FQ):
                    P(lambda e, fc=fc: e.memset(
                        qkvraw[:, fc * (TQ + 3):fc * (TQ + 3) + 3], 0.0))
                halo_g = cg.n
            names = ("pb3", "pb4")
            bks = (pb[3], pb[4])
            kk = 0
            for fc in range(FQ):
                nm = names[kk % 2]
                ps = bks[kk % 2]
                kk += 1
                use("t", nm)
                for chk in range(KC):
                    T(lambda e, fc=fc, chk=chk, ps=ps, par=par: e.matmul(
                        ps[:, 0:512],
                        wqkv[:, chk * FQ * 128 + fc * 128:
                             chk * FQ * 128 + (fc + 1) * 128],
                        hT[par][:, chk * TQ: chk * TQ + 512],
                        start=(chk == 0), stop=(chk == KC - 1)))
                mmn = ct.n
                rec.wait("a", ct, mmn)
                A(lambda e, fc=fc, ps=ps: e.activation(
                    qkvraw[:, fc * (TQ + 3) + 3: fc * (TQ + 3) + 3 + 512],
                    ps[:, 0:512], AF.Copy))
                setr(nm, ca)
                yield
            if first:
                rec.wait("t", cg, halo_g)
            for fc in range(FQ):
                nm = names[kk % 2]
                ps = bks[kk % 2]
                kk += 1
                use("t", nm)
                for j in range(4):
                    T(lambda e, fc=fc, j=j, ps=ps: e.matmul(
                        ps[:, 0:512],
                        convd[:, (fc * 4 + j) * 128:(fc * 4 + j + 1) * 128],
                        qkvraw[:, fc * (TQ + 3) + j: fc * (TQ + 3) + j + 512],
                        start=(j == 0), stop=(j == 3)))
                mmn = ct.n
                rec.wait("a", ct, mmn)
                A(lambda e, fc=fc, ps=ps, par=par: e.activation(
                    qkvs[par][:, fc * TQ: fc * TQ + 512],
                    ps[:, 0:512], AF.Silu))
                setr(nm, ca)
                yield
            W("g", ca)
            for fc in range(FQ):
                P(lambda e, fc=fc: e.tensor_copy(
                    qkvraw[:, fc * (TQ + 3):fc * (TQ + 3) + 3],
                    qkvraw[:, fc * (TQ + 3) + TQ:fc * (TQ + 3) + TQ + 3]))
            for ts in range(NSUB):
                nm = names[kk % 2]
                psz = bks[kk % 2]
                kk += 1
                use("t", nm)
                for chk in range(KC):
                    T(lambda e, ts=ts, chk=chk, psz=psz, par=par: e.matmul(
                        psz[:, 0:ZW],
                        hT[par][:, chk * TQ + ts * 128:
                                chk * TQ + (ts + 1) * 128],
                        wzba[:, chk * ZW:(chk + 1) * ZW],
                        start=(chk == 0), stop=(chk == KC - 1)))
                mmn = ct.n
                rec.wait("a", ct, mmn)
                A(lambda e, ts=ts, psz=psz, par=par: e.activation(
                    zsil[par][:, ts * HPC * 128:(ts + 1) * HPC * 128],
                    psz[:, 0:HPC * 128], AF.Silu))
                for hh in range(HPC):
                    for s_i, slot in ((0, 0), (1, 1)):
                        A(lambda e, ts=ts, hh=hh, s_i=s_i, slot=slot, psz=psz,
                          par=par: e.activation(
                            mats[par][:, (hh * MS + slot) * NSUB + ts:
                                 (hh * MS + slot) * NSUB + ts + 1],
                            psz[:, HPC * 128 + 2 * hh + s_i:
                                HPC * 128 + 2 * hh + s_i + 1], AF.Copy))
                setr(nm, ca)
                yield
            setr(f"hTfree{par}", ct)
            for hh in range(HPC):
                for ts in range(NSUB):
                    for part, dstb in ((1, ktm), (2, vtm)):
                        nm = names[kk % 2]
                        ps = bks[kk % 2]
                        kk += 1
                        tr = ps[:, 0:64].bitcast(BF16)
                        use("t", nm)
                        T(lambda e, hh=hh, ts=ts, part=part, par=par,
                          tr=tr: e.transpose(
                            tr[:, 0:128],
                            qkvs[par][:, (3 * hh + part) * TQ + ts * 128:
                                      (3 * hh + part) * TQ + (ts + 1) * 128],
                            G.s_identb[:, :]))
                        mmn = ct.n
                        rec.wait("a", ct, mmn)
                        A(lambda e, hh=hh, ts=ts, dstb=dstb, par=par,
                          tr=tr: e.activation(
                            dstb[par][:, hh * TQ + ts * 128:
                                      hh * TQ + (ts + 1) * 128],
                            tr[:, 0:128], AF.Copy))
                        setr(nm, ca)
                        if kk % 2 == 0:
                            yield

        def mats_prep_gen(b, i, par):
            colf = lambda hh, s: mats[par][:, (hh * MS + s) * NSUB:
                                          (hh * MS + s + 1) * NSUB]
            W("a", ca)
            W("a", cv)
            W("a", cg)
            for hh in range(HPC):
                A(lambda e, hh=hh: e.activation(colf(hh, 2), colf(hh, 0),
                                                AF.Sigmoid))
            W("a", ca)
            for hh in range(HPC):
                A(lambda e, hh=hh: e.activation(colf(hh, 3), colf(hh, 2),
                                                AF.Copy, scale=-1.0))
                A(lambda e, hh=hh: e.activation(colf(hh, 11), colf(hh, 1),
                                                AF.Abs,
                                                bias=headc[:, hh:hh + 1]))
            W("a", ca)
            for hh in range(HPC):
                A(lambda e, hh=hh: e.activation(colf(hh, 11), colf(hh, 11),
                                                AF.Exp, scale=-1.0))
            W("a", ca)
            b_one = G.cst(1.0)
            for hh in range(HPC):
                A(lambda e, hh=hh, bo=b_one: e.activation(
                    colf(hh, 11), colf(hh, 11), AF.Ln, bias=bo))
            W("v", ca)
            for hh in range(HPC):
                V(lambda e, hh=hh: e.tensor_scalar(
                    colf(hh, 12), colf(hh, 1), headc[:, hh:hh + 1], 0.0,
                    ALU.add, ALU.max))
            W("v", cv)
            for hh in range(HPC):
                V(lambda e, hh=hh: e.tensor_add(colf(hh, 12), colf(hh, 12),
                                                colf(hh, 11)))
            W("v", cv)
            for hh in range(HPC):
                V(lambda e, hh=hh: e.tensor_scalar_mul(
                    colf(hh, 10), colf(hh, 12),
                    headc[:, HPC + hh:HPC + hh + 1]))
            yield
            use("t", "pb3", "pb4")
            W("t", cv)
            for hh in range(HPC):
                T(lambda e, hh=hh: e.matmul(
                    pb[3][:, hh * NSUB:(hh + 1) * NSUB], G.s_triu1_s[:, :],
                    colf(hh, 10), start=True, stop=True))
            mmn = ct.n
            rec.wait("a", ct, mmn)
            for hh in range(HPC):
                A(lambda e, hh=hh: e.activation(
                    colf(hh, 4), pb[3][:, hh * NSUB:(hh + 1) * NSUB],
                    AF.Copy))
            W("a", ca)
            for hh in range(HPC):
                A(lambda e, hh=hh: e.activation(colf(hh, 5), colf(hh, 4),
                                                AF.Copy, scale=-1.0))
                A(lambda e, hh=hh: e.activation(colf(hh, 6), colf(hh, 4),
                                                AF.Exp))
            yield
            W("t", ca)
            for hh in range(HPC):
                T(lambda e, hh=hh: e.matmul(
                    pb[3][:, (HPC + hh) * NSUB:(HPC + hh + 1) * NSUB],
                    G.s_e127_s[:, :], colf(hh, 4), start=True, stop=True))
            mmn = ct.n
            rec.wait("a", ct, mmn)
            rec.wait("v", ct, mmn)
            for hh in range(HPC):
                A(lambda e, hh=hh: e.activation(
                    colf(hh, 8), pb[3][:, (HPC + hh) * NSUB:
                                       (HPC + hh + 1) * NSUB], AF.Exp))
                V(lambda e, hh=hh: e.tensor_sub(
                    colf(hh, 12), pb[3][:, (HPC + hh) * NSUB:
                                        (HPC + hh + 1) * NSUB], colf(hh, 4)))
            W("a", cv)
            for hh in range(HPC):
                A(lambda e, hh=hh: e.activation(colf(hh, 7), colf(hh, 12),
                                                AF.Exp))
            W("v", ca)
            for hh in range(HPC):
                V(lambda e, hh=hh: e.tensor_mul(colf(hh, 9), colf(hh, 3),
                                                colf(hh, 6)))
            yield
            W("t", cv)
            for hh in range(HPC):
                T(lambda e, hh=hh: e.transpose(
                    pb[4][0:2 * NSUB, 128 + 128 * hh:256 + 128 * hh],
                    mats[par][:, (hh * MS + 3) * NSUB:(hh * MS + 5) * NSUB],
                    G.s_identf_s[:, :]))
            mmn = ct.n
            rec.wait("a", ct, mmn)
            W("a", crw)
            for hh in range(HPC):
                A(lambda e, hh=hh: e.activation(
                    rbt2[hh][0:2 * NSUB, :],
                    pb[4][0:2 * NSUB, 128 + 128 * hh:256 + 128 * hh],
                    AF.Copy))
            an = ca.n
            rec.wait("s", ca, an)
            for hh in range(HPC):
                roff = ((b * NT + i) * HPC + hh) * 2 * NSUB
                rec.op("s", lambda e, roff=roff, hh=hh: e.dma_start(
                    out=G.rowscr[roff:roff + 2 * NSUB, :],
                    in_=rbt2[hh][0:2 * NSUB, :]), dma=crw)
            wn_ = crw.n
            rec.wait("s", crw, wn_)
            W("s", cg)
            W("s", ct)
            for hh in range(HPC):
                roff = ((b * NT + i) * HPC + hh) * 2 * NSUB
                rec.op("s", lambda e, roff=roff, hh=hh: e.dma_start(
                    out=rowb[0:1, hh * 3 * TQ: hh * 3 * TQ + TQ],
                    in_=G.rowscr[roff:roff + NSUB, :].rearrange(
                        "r k -> (r k)").unsqueeze(0)), dma=crw)
                rec.op("s", lambda e, roff=roff, hh=hh: e.dma_start(
                    out=rowb[0:1, hh * 3 * TQ + TQ: hh * 3 * TQ + 2 * TQ],
                    in_=G.rowscr[roff + NSUB:roff + 2 * NSUB, :].rearrange(
                        "r k -> (r k)").unsqueeze(0)), dma=crw)
            rows_in = crw.n
            yield
            sq_done = ct.n
            W("a", cg)
            for hh in range(HPC):
                for part in (0, 1):
                    rec.wait("a", ct, sq_done)
                    W("a", cv)
                    A(lambda e, hh=hh, part=part, par=par: e.activation(
                        rowsq[:, :],
                        qkvs[par][:, (3 * hh + part) * TQ:
                                  (3 * hh + part + 1) * TQ],
                        AF.Square))
                    an2 = ca.n
                    rec.wait("t", ca, an2)
                    T(lambda e: e.matmul(
                        pb[4][0:1, 0:TQ], G.s_ones_col_b[:, :],
                        rowsq[:, 0:TQ], start=True, stop=True))
                    sq_done = ct.n
                    rec.wait("a", ct, sq_done)
                    ro = part * 512
                    if part == 0:
                        bq_ = G.cst(128e-6)[0:1, :]
                        A(lambda e, b_=bq_, ro=ro: e.activation(
                            rowt[0:1, ro:ro + TQ], pb[4][0:1, 0:TQ],
                            AF.Sqrt, scale=128.0, bias=b_))
                    else:
                        bk_ = G.cst(1e-6)[0:1, :]
                        A(lambda e, b_=bk_, ro=ro: e.activation(
                            rowt[0:1, ro:ro + TQ], pb[4][0:1, 0:TQ],
                            AF.Sqrt, bias=b_))
                    an3 = ca.n
                    rec.wait("v", ca, an3)
                    if part == 0:
                        V(lambda e, hh=hh, ro=ro: e.reciprocal(
                            rowft[0:1, hh * TQ:(hh + 1) * TQ],
                            rowt[0:1, ro:ro + TQ]))
                    else:
                        V(lambda e, hh=hh, ro=ro: e.reciprocal(
                            rowbt[0:1, hh * TQ:(hh + 1) * TQ],
                            rowt[0:1, ro:ro + TQ]))
                    vn_ = cv.n
                    if part == 1:
                        rec.wait("t", cv, vn_)
                        for cc8 in range(NSUB):
                            T(lambda e, hh=hh, cc8=cc8: e.matmul(
                                pb[4][:, 2 + hh * NSUB + cc8:
                                      3 + hh * NSUB + cc8],
                                rowbt[0:1, hh * TQ + cc8 * 128:
                                      hh * TQ + (cc8 + 1) * 128],
                                G.s_one11_f[:, :], start=True, stop=True))
                        mmn2 = ct.n
                        rec.wait("a", ct, mmn2)
                        A(lambda e, hh=hh: e.activation(
                            rkcol[:, hh * NSUB:(hh + 1) * NSUB],
                            pb[4][:, 2 + hh * NSUB: 2 + (hh + 1) * NSUB],
                            AF.Copy))
                        rec.wait("t", ca, ca.n)
                    yield
            W("g", cv)
            W("g", ct)
            for hh in range(HPC):
                P(lambda e, hh=hh: e.tensor_copy(
                    rowbb[0:1, hh * 3 * TQ: hh * 3 * TQ + TQ],
                    rowft[0:1, hh * TQ:(hh + 1) * TQ]))
            rec.wait("a", crw, rows_in)
            for hh in range(HPC):
                A(lambda e, hh=hh: e.activation(
                    rowb[0:1, hh * 3 * TQ + 2 * TQ: hh * 3 * TQ + 3 * TQ],
                    rowb[0:1, hh * 3 * TQ + TQ: hh * 3 * TQ + 2 * TQ],
                    AF.Exp))
            an4 = ca.n
            rec.wait("g", ca, an4)
            rec.wait("g", crw, rows_in)
            for hh in range(HPC):
                P(lambda e, hh=hh: e.tensor_mul(
                    rowbb[0:1, hh * 3 * TQ + TQ: hh * 3 * TQ + 2 * TQ],
                    rowb[0:1, hh * 3 * TQ: hh * 3 * TQ + TQ],
                    rowbt[0:1, hh * TQ:(hh + 1) * TQ]))
                P(lambda e, hh=hh: e.tensor_mul(
                    rowbb[0:1, hh * 3 * TQ + 2 * TQ: hh * 3 * TQ + 3 * TQ],
                    rowb[0:1, hh * 3 * TQ + 2 * TQ: hh * 3 * TQ + 3 * TQ],
                    rowft[0:1, hh * TQ:(hh + 1) * TQ]))
            setr("pb3", ca, cv)
            setr("pb4", ca, cv)
            setr("rows", ca, cg, cv)

        def prescan_gen(b, i, par, c8, hh):
            p2 = c8 % 2
            sl2 = p2 * HPC + hh
            base = hh * MS * NSUB
            col1 = lambda s, base=base, c8=c8, par=par: mats[par][
                :, base + s * NSUB + c8: base + s * NSUB + c8 + 1]
            qo = (3 * hh + 0) * TQ + c8 * 128
            ko = (3 * hh + 1) * TQ + c8 * 128
            qT_c = qkvs[par][:, qo:qo + 128]
            kT_c = qkvs[par][:, ko:ko + 128]
            ktm_c = ktm[par][:, hh * TQ + c8 * 128: hh * TQ + (c8 + 1) * 128]
            vtm_c = vtm[par][:, hh * TQ + c8 * 128: hh * TQ + (c8 + 1) * 128]
            bank = DB[sl2]
            btr = DBtr[sl2]
            bname = f"dbank{sl2}"
            slname = f"slot{sl2}"
            shname = f"shared{sl2}"
            rkc = rkcol[:, hh * NSUB + c8: hh * NSUB + c8 + 1]
            rrow = lambda s: rowb[0:1, hh * 3 * TQ + s * TQ + c8 * 128:
                                  hh * 3 * TQ + s * TQ + (c8 + 1) * 128]
            brow = lambda s: rowbb[0:1, hh * 3 * TQ + s * TQ + c8 * 128:
                                   hh * 3 * TQ + s * TQ + (c8 + 1) * 128]
            so = sl2 * 128
            so2 = sl2 * 256
            mo_ = sl2 * 128
            eo = sl2 * 256
            uo = sl2 * NLV * 128
            to = sl2 * 256
            xo0 = sl2 * 512
            ao = sl2 * 256
            use("t", bname, "rows")
            T(lambda e: e.matmul(bank[:, 0:128], G.s_ones_row_f[:1, :128],
                                 rrow(1), start=True, stop=True))
            T(lambda e: e.matmul(bank[:, 128:256], G.s_ones_row_b[:1, :128],
                                 brow(0), start=True, stop=True))
            T(lambda e: e.matmul(bank[:, 256:384], G.s_ones_row_b[:1, :128],
                                 brow(1), start=True, stop=True))
            T(lambda e: e.matmul(bank[:, 384:512], G.s_ones_row_b[:1, :128],
                                 brow(2), start=True, stop=True))
            use("t", "pb4")
            T(lambda e: e.matmul(pb[4][:, 0:128], kT_c, qT_c,
                                 start=True, stop=True))
            T(lambda e: e.matmul(pb[4][:, 128:256], kT_c, kT_c,
                                 start=True, stop=True))
            au_mm = ct.n
            use("a", shname)
            rec.wait("a", ct, au_mm)
            A(lambda e: e.activation(rawAU[:, ao:ao + 128], pb[4][:, 0:128],
                                     AF.Copy))
            A(lambda e: e.activation(rawAU[:, ao + 128:ao + 256],
                                     pb[4][:, 128:256], AF.Copy))
            setr("pb4", ca)
            bc_done = ct.n
            yield
            use("v", shname)
            rec.wait("v", ct, bc_done)
            V(lambda e: e.tensor_add(Mb[:, mo_:mo_ + 128], bank[:, 0:128],
                                     G.s_mask_us[:, :]))
            v1_ = cv.n
            rec.wait("a", cv, v1_)
            A(lambda e: e.activation(Eb[:, eo:eo + 128], Mb[:, mo_:mo_ + 128],
                                     AF.Exp, bias=col1(5)))
            a1_ = ca.n
            rec.wait("v", ca, a1_)
            V(lambda e: e.tensor_add(Mb[:, mo_:mo_ + 128], bank[:, 0:128],
                                     G.s_mask_ui[:, :]))
            v2_ = cv.n
            rec.wait("a", cv, v2_)
            A(lambda e: e.activation(Eb[:, eo + 128:eo + 256],
                                     Mb[:, mo_:mo_ + 128], AF.Exp,
                                     bias=col1(5)))
            a2_ = ca.n
            yield
            use("v", slname)
            rec.wait("v", ca, a2_)
            V(lambda e: e.scalar_tensor_tensor(
                mUb[:, mo_:mo_ + 128], Eb[:, eo:eo + 128], rkc,
                bank[:, 256:384], ALU.mult, ALU.mult))
            V(lambda e: e.scalar_tensor_tensor(
                mab[:, mo_:mo_ + 128], Eb[:, eo + 128:eo + 256], rkc,
                bank[:, 128:256], ALU.mult, ALU.mult))
            v3_ = cv.n
            rec.wait("v", cv, v3_)
            V(lambda e: e.tensor_mul(Upow[:, uo:uo + 128],
                                     rawAU[:, ao + 128:ao + 256],
                                     mUb[:, mo_:mo_ + 128]))
            V(lambda e: e.tensor_mul(attT[:, so:so + 128],
                                     rawAU[:, ao:ao + 128],
                                     mab[:, mo_:mo_ + 128]))
            V(lambda e: e.tensor_mul(qgT[:, so:so + 128], qT_c,
                                     bank[:, 384:512]))
            ub_done = cv.n
            yield
            rec.wait("t", cv, ub_done)
            T(lambda e: e.transpose(btr[:, 0:128], Upow[:, uo:uo + 128],
                                    G.s_identb[:, :]))
            t4 = ct.n
            rec.wait("a", ct, t4)
            A(lambda e: e.activation(Tpow[:, to:to + 128], btr[:, 0:128],
                                     AF.Copy))
            lvla = ca.n
            yield
            for lv in range(1, NLV):
                up = Upow[:, uo + (lv - 1) * 128: uo + lv * 128]
                tp = Tpow[:, to + ((lv - 1) % 2) * 128:
                          to + ((lv - 1) % 2 + 1) * 128]
                rec.wait("t", ca, lvla)
                T(lambda e, up=up, tp=tp: e.matmul(
                    bank[:, 0:128], tp, up, start=True, stop=True))
                if lv < NLV - 1:
                    T(lambda e, up=up, tp=tp: e.matmul(
                        bank[:, 128:256], up, tp, start=True, stop=True))
                mmn = ct.n
                rec.wait("a", ct, mmn)
                A(lambda e, lv=lv: e.activation(
                    Upow[:, uo + lv * 128:uo + (lv + 1) * 128],
                    bank[:, 0:128], AF.Copy))
                if lv < NLV - 1:
                    A(lambda e, lv=lv: e.activation(
                        Tpow[:, to + (lv % 2) * 128:to + (lv % 2 + 1) * 128],
                        bank[:, 128:256], AF.Copy))
                lvla = ca.n
                if lv % 2 == 0:
                    yield
            yield
            use("g", slname, shname, "rows")
            W("g", cg)
            P(lambda e: e.tensor_mul(colsc[:, sl2 * 8 + 1:sl2 * 8 + 2],
                                     col1(9), rkc))
            P(lambda e: e.tensor_mul(colsc[:, sl2 * 8 + 2:sl2 * 8 + 3],
                                     col1(7), rkc))
            g1_ = cg.n
            rec.wait("g", cg, g1_)
            P(lambda e: e.tensor_scalar_mul(Xb[:, xo0:xo0 + 128], vtm_c,
                                            col1(2)))
            P(lambda e: e.tensor_scalar_mul(
                Xb[:, xo0 + 128:xo0 + 256], ktm_c,
                colsc[:, sl2 * 8 + 1:sl2 * 8 + 2]))
            P(lambda e: e.tensor_scalar_mul(
                kdec[:, so:so + 128], ktm_c,
                colsc[:, sl2 * 8 + 2:sl2 * 8 + 3]))
            g2_ = cg.n
            yield
            rec.wait("t", ca, lvla)
            rec.wait("t", cg, g2_)
            prev_add = 0
            for lv in range(NLV):
                xs = xo0 + (lv % 2) * 256
                xd = xo0 + ((lv + 1) % 2) * 256
                rec.wait("t", cv, prev_add)
                T(lambda e, lv=lv, xs=xs: e.matmul(
                    bank[:, 0:256],
                    Upow[:, uo + lv * 128:uo + (lv + 1) * 128],
                    Xb[:, xs:xs + 256], start=True, stop=True))
                mmn = ct.n
                rec.wait("v", ct, mmn)
                if lv < NLV - 1:
                    V(lambda e, xs=xs, xd=xd: e.tensor_add(
                        Xb[:, xd:xd + 256], bank[:, 0:256],
                        Xb[:, xs:xs + 256]))
                else:
                    V(lambda e, xs=xs: e.tensor_add(
                        Xfin[:, so2:so2 + 256], bank[:, 0:256],
                        Xb[:, xs:xs + 256]))
                prev_add = cv.n
                if lv == 3:
                    yield
            yield
            use("a", slname)
            rec.wait("t", cv, prev_add)
            T(lambda e: e.transpose(btr[:, 0:128],
                                    Xfin[:, so2 + 128:so2 + 256],
                                    G.s_identb[:, :]))
            t8 = ct.n
            rec.wait("a", ct, t8)
            A(lambda e: e.activation(wnT[:, so:so + 128], btr[:, 0:128],
                                     AF.Copy))
            setr(bname, ca, cv)
            setr(shname, ct, cv, ca, cg)
            setr(f"pres{sl2}", ct, cv, ca, cg)

        def scan_gen(b, i, par, c8):
            p2 = c8 % 2
            for hh in range(HPC):
                sl2 = p2 * HPC + hh
                base = hh * MS * NSUB
                col1 = lambda s, base=base, c8=c8, par=par: mats[par][
                    :, base + s * NSUB + c8: base + s * NSUB + c8 + 1]
                so = sl2 * 128
                so2 = sl2 * 256
                ho = hh * 128
                Sb_h = Sb[:, ho:ho + 128]
                Sf_h = Sf[:, ho:ho + 128]
                ssn = f"sst{hh}"
                use("t", f"pres{sl2}", ssn, "pb7")
                T(lambda e, so=so, Sb_h=Sb_h: e.matmul(
                    pb[7][:, 0:128], wnT[:, so:so + 128], Sb_h,
                    start=True, stop=True))
                vready = ct.n
                rec.wait("v", ct, vready)
                use("v", ssn)
                V(lambda e, so2=so2, ho=ho: e.tensor_add(
                    vnew[:, ho:ho + 128], pb[7][:, 0:128],
                    Xfin[:, so2:so2 + 128]))
                vn_done = cv.n
                yield
                use("t", "pb0")
                T(lambda e, so=so, Sb_h=Sb_h: e.matmul(
                    pb[0][:, 0:128], qgT[:, so:so + 128], Sb_h,
                    start=True, stop=False))
                rec.wait("t", cv, vn_done)
                T(lambda e, so=so, ho=ho: e.matmul(
                    pb[0][:, 0:128], attT[:, so:so + 128],
                    vnew[:, ho:ho + 128], start=False, stop=True))
                T(lambda e, so=so, ho=ho: e.matmul(
                    pb[7][:, 128:256], kdec[:, so:so + 128],
                    vnew[:, ho:ho + 128], start=True, stop=True))
                smm = ct.n
                rec.wait("v", ct, smm)
                V(lambda e, Sf_h=Sf_h, col1=col1: e.scalar_tensor_tensor(
                    Sf_h, Sf_h, col1(8), pb[7][:, 128:256],
                    ALU.mult, ALU.add))
                setr("pb7", cv)
                sup = cv.n
                rec.wait("a", cv, sup)
                A(lambda e, Sb_h=Sb_h, Sf_h=Sf_h: e.activation(
                    Sb_h, Sf_h, AF.Copy))
                yield
                A(lambda e, ho=ho: e.activation(osq[:, ho:ho + 128],
                                                pb[0][:, 0:128], AF.Square))
                osq_a = ca.n
                rec.wait("v", ca, osq_a)
                go = hh * 4
                V(lambda e, go=go, ho=ho: e.reduce_sum(
                    oc2[:, go:go + 1], osq[:, ho:ho + 128], AX.X))
                rr1 = cv.n
                rec.wait("a", cv, rr1)
                b_e6 = G.cst(1e-6)
                A(lambda e, b=b_e6, go=go: e.activation(
                    oc2[:, go + 1:go + 2], oc2[:, go:go + 1], AF.Sqrt,
                    scale=1.0 / 128, bias=b))
                rr2 = ca.n
                rec.wait("v", ca, rr2)
                V(lambda e, go=go: e.reciprocal(oc2[:, go + 2:go + 3],
                                                oc2[:, go + 1:go + 2]))
                rr3 = cv.n
                rec.wait("v", cv, rr3)
                V(lambda e, go=go, ho=ho: e.scalar_tensor_tensor(
                    ot1[:, ho:ho + 128], pb[0][:, 0:128],
                    oc2[:, go + 2:go + 3], G.s_wnorm_bc[:, :],
                    ALU.mult, ALU.mult))
                setr("pb0", cv, ca)
                use("v", "onorm")
                rr4 = cv.n
                rec.wait("v", cv, rr4)
                V(lambda e, par=par, c8=c8, ho=ho: e.tensor_mul(
                    onorm[:, ho:ho + 128], ot1[:, ho:ho + 128],
                    zsil[par][:, c8 * HPC * 128 + ho:
                              c8 * HPC * 128 + ho + 128]))
                setr(ssn, ct, cv, ca)
                yield

        def store_chunk(b, i, c8):
            W("s", cv)
            r0 = b * c.S + i * TQ + c8 * 128
            rec.op("s", lambda e, r0=r0: e.dma_start(
                out=G.agin_na[r0:r0 + 128, :], in_=onorm[:, :]), dma=cso)
            setr("onorm", cso)

        tiles = [(b, i) for b in range(c.B) for i in range(NT)]
        load_hT(0, 0, 0)
        for _ in stage_a_gen(0, 0, 0, first=True):
            pass
        for _ in mats_prep_gen(0, 0, 0):
            pass
        if len(tiles) > 1:
            load_hT(tiles[1][0], tiles[1][1], 1)
        sa = None
        mp = None
        for k, (b, i) in enumerate(tiles):
            par = k % 2
            if i == 0:
                W("v", ct)
                W("v", ca)
                V(lambda e: e.memset(Sf[:, :], 0.0))
                V(lambda e: e.memset(Sb[:, :], 0.0))
            while mp is not None:
                try:
                    next(mp)
                except StopIteration:
                    mp = None
            if k + 1 < len(tiles):
                bn, in_ = tiles[k + 1]
                sa = stage_a_gen(bn, in_, (k + 1) % 2, first=(in_ == 0))
                mp = mats_prep_gen(bn, in_, (k + 1) % 2)
            if k + 2 < len(tiles):
                b2, i2 = tiles[k + 2]
                use("s", f"hTfree{k % 2}")
                load_hT(b2, i2, k % 2)
            for c8 in range(NSUB + 1):
                gens = []
                if c8 < NSUB:
                    gens += [prescan_gen(b, i, par, c8, hh)
                             for hh in range(HPC)]
                if c8 > 0:
                    gens.append(scan_gen(b, i, par, c8 - 1))
                while gens:
                    for g_ in list(gens):
                        try:
                            next(g_)
                        except StopIteration:
                            gens.remove(g_)
                    if sa is not None:
                        try:
                            next(sa)
                        except StopIteration:
                            sa = None
                    elif mp is not None and c8 == NSUB:
                        try:
                            next(mp)
                        except StopIteration:
                            mp = None
                if c8 > 0:
                    cp = c8 - 1
                    for hh in range(HPC):
                        setr(f"slot{(cp % 2) * HPC + hh}", ct)
                    store_chunk(b, i, cp)
            while sa is not None:
                try:
                    next(sa)
                except StopIteration:
                    sa = None
            setr(f"tdone{par}", ct, cv, ca, cg)
            if k == 1:
                emit_casts()
        W("g", cso)

    rec.op("g", lambda e: e.collective_compute(
        "AllGather", ALU.bypass, replica_groups=[list(range(c.NC))],
        ins=[G.agin_na[:, :]], outs=[G.ag_na[:, :]]), tick=ch("cc_na"))
    G.dbg("na", G.ag_na, after_ch=ch("cc_na"))


def p_mixer_out(G, tag, ag, my, cc, w_direct, bias, x_src, x_dst, h_dst, dep):
    c, nc, rec, ch = G.c, G.nc, G.rec, G.ch
    Tc, D, KC, HPC = c.Tc, c.D, c.KC, c.HPC
    with ExitStack() as S:
        def sb(n, shape, dt):
            return S.enter_context(nc.sbuf_tensor(tag + "_" + n, list(shape), dt))

        nT = sb("nT", (128, KC * Tc), BF16)
        wsb = sb("w", (128, KC * D), BF16)
        xa = sb("xa", (128, D), F32)
        xb = sb("xb", (128, D), F32)
        xt = sb("xt", (128, D), F32)
        hb = sb("hb", (128, D), BF16)
        sq = sb("sq", (128, D), F32)
        cols = sb("cols", (128, 8), F32)
        bo_sb = sb("bo", (128, D), F32) if bias is not None else None
        bo_row = sb("borow", (1, D), F32) if bias is not None else None
        pp = [S.enter_context(nc.psum_tensor(tag + f"_p{i}", [128, 512], F32))
              for i in range(3)]

        ccp = ch(tag + "_cp")
        ctr = ch(tag + "_tr")
        cwl = ch(tag + "_wl")
        cxl = ch(tag + "_xl")
        cst = ch(tag + "_st")
        ct = ch(tag + "_t")
        cv = ch(tag + "_v")
        ca = ch(tag + "_a")
        W = rec.wait

        if dep is not None:
            W("s", dep)
        W("s", ch(cc))
        rec.op("s", lambda e: e.dma_start(
            out=my.rearrange("t (r x) -> r t x", r=c.NC),
            in_=ag.rearrange("(r t) x -> r t x", r=c.NC)
                [:, bass.ds(e.partition_id() * Tc, Tc), :]), dma=ccp)
        W("s", ccp)
        for dch in range(KC):
            rec.op("s", lambda e, dch=dch: e.dma_start(
                out=nT[:, dch * Tc:(dch + 1) * Tc],
                in_=my[0:Tc, dch * 128:(dch + 1) * 128],
                transpose=True), dma=ctr)
        if w_direct is not None:
            rec.op("g", lambda e: e.dma_start(
                out=wsb[:, :].rearrange("p (ch f) -> p ch f", f=D),
                in_=w_direct.rearrange("(ch p) f -> p ch f", p=128)), dma=cwl)
        else:
            W("a", ch("c_wprep"))
            for ech in range(KC):
                rec.op("a", lambda e, ech=ech: e.dma_start(
                    out=wsb[:, ech * D:(ech + 1) * D],
                    in_=G.wo_b[0:D, ech * 128:(ech + 1) * 128],
                    transpose=True), dma=cwl)
        if bias is not None:
            cbl = ch(tag + "_bl")
            rec.op("s", lambda e: e.dma_start(out=bo_row[:, :], in_=bias),
                   dma=cbl)
            W("t", cbl)
            for f in range(D // 512):
                rec.op("t", lambda e, f=f: e.matmul(
                    pp[2][:, 0:512], G.s_ones_row_f[:1, :128],
                    bo_row[0:1, f * 512:(f + 1) * 512],
                    start=True, stop=True), tick=ct)
                W("v", ct)
                rec.op("v", lambda e, f=f: e.tensor_copy(
                    bo_sb[:, f * 512:(f + 1) * 512], pp[2][:, 0:512]),
                    tick=cv)
                W("t", cv)

        b_eps = G.cst(1e-5)
        for tt in range(Tc // 128):
            xo = (xa, xb)[tt % 2]
            rec.op("s", lambda e, tt=tt, xo=xo: e.dma_start(
                out=xo[:, :], in_=x_src[tt * 128:(tt + 1) * 128, :]), dma=cxl)
            for f in range(D // 512):
                ps = pp[f % 2]
                W("t", ctr)
                W("t", cwl)
                W("t", cv)
                for dch in range(KC):
                    rec.op("t", lambda e, tt=tt, f=f, dch=dch, ps=ps: e.matmul(
                        ps[:, 0:512],
                        nT[:, dch * Tc + tt * 128: dch * Tc + (tt + 1) * 128],
                        wsb[:, dch * D + f * 512: dch * D + (f + 1) * 512],
                        start=(dch == 0), stop=(dch == KC - 1)), tick=ct)
                W("v", ct)
                W("v", cxl)
                W("v", ca)
                rec.op("v", lambda e, f=f, ps=ps, xo=xo: e.tensor_add(
                    xt[:, f * 512:(f + 1) * 512], ps[:, 0:512],
                    xo[:, f * 512:(f + 1) * 512]), tick=cv)
                if bias is not None:
                    W("v", cv)
                    rec.op("v", lambda e, f=f: e.tensor_add(
                        xt[:, f * 512:(f + 1) * 512],
                        xt[:, f * 512:(f + 1) * 512],
                        bo_sb[:, f * 512:(f + 1) * 512]), tick=cv)
            W("v", cv)
            W("v", cst)
            rec.op("v", lambda e: e.reduce_sum(cols[:, 0:1], xt[:, :], AX.X),
                   tick=cv)
            W("a", cv)
            rec.op("a", lambda e: e.activation(cols[:, 1:2], cols[:, 0:1],
                                               AF.Copy, scale=1.0 / D), tick=ca)
            W("v", ca)
            rec.op("v", lambda e: e.tensor_scalar_sub(xt[:, :], xt[:, :],
                                                      cols[:, 1:2]), tick=cv)
            W("a", cv)
            rec.op("a", lambda e: e.activation(sq[:, :], xt[:, :], AF.Square),
                   tick=ca)
            W("v", ca)
            rec.op("v", lambda e: e.reduce_sum(cols[:, 2:3], sq[:, :], AX.X),
                   tick=cv)
            W("a", cv)
            rec.op("a", lambda e, b=b_eps: e.activation(
                cols[:, 3:4], cols[:, 2:3], AF.Sqrt, scale=1.0 / D, bias=b),
                tick=ca)
            W("v", ca)
            rec.op("v", lambda e: e.reciprocal(cols[:, 2:3], cols[:, 3:4]),
                   tick=cv)
            W("v", cv)
            rec.op("v", lambda e: e.tensor_scalar_mul(hb[:, :], xt[:, :],
                                                      cols[:, 2:3]), tick=cv)
            W("v", cv)
            rec.op("v", lambda e: e.tensor_scalar_add(xt[:, :], xt[:, :],
                                                      cols[:, 1:2]), tick=cv)
            W("s", cv)
            rec.op("s", lambda e, tt=tt: e.dma_start(
                out=x_dst[tt * 128:(tt + 1) * 128, :], in_=xt[:, :]), dma=cst)
            rec.op("s", lambda e, tt=tt: e.dma_start(
                out=h_dst[tt * 128:(tt + 1) * 128, :], in_=hb[:, :]), dma=cst)
        W("s", cst)
        W("g", cst)


def p_ffn(G, tag, h_src, x_src, x_dst, h_dst, last, dep):
    c, nc, rec, ch = G.c, G.nc, G.rec, G.ch
    Tc, D, KC, E, F, FS = c.Tc, c.D, c.KC, c.E, c.F, c.FS
    TP = c.FFN_TP
    NP = Tc // TP
    NTS = TP // 128
    FH = max(128, F // 2)
    FSH = min(512, FS)
    FMX = max(FH, FSH)
    with ExitStack() as S:
        def sb(n, shape, dt):
            return S.enter_context(nc.sbuf_tensor(tag + "_" + n, list(shape), dt))

        xT = sb("xT", (128, KC * TP), BF16)
        rout = sb("rout", (128, KC * E), BF16)
        gates = sb("gates", (128, NTS * E), F32)
        sc = sb("sc", (128, E), F32)
        sc2 = sb("sc2", (128, E), F32)
        sc3 = sb("sc3", (128, E), F32)
        gcol = sb("gcol", (128, 8), F32)
        w1s = [sb("w1a", (128, KC * FMX), BF16), sb("w1b", (128, KC * FMX), BF16)]
        v1s = [sb("v1a", (128, KC * FMX), BF16), sb("v1b", (128, KC * FMX), BF16)]
        w2s = [sb("w2a", (128, (FMX // 128) * D), BF16),
               sb("w2b", (128, (FMX // 128) * D), BF16)]
        heT = sb("heT", (128, (FMX // 128) * TP), BF16)
        st = sb("st", (128, 1024), F32)
        acc = sb("acc", (128, NTS * D), F32)
        xa = sb("xa", (128, D), F32)
        xb = sb("xb", (128, D), F32)
        xt = sb("xt", (128, D), F32)
        hb = sb("hb", (128, D), BF16)
        sq = sb("sq", (128, D), F32)
        cols = sb("cols", (128, 8), F32)
        pp = [S.enter_context(nc.psum_tensor(tag + f"_p{i}", [128, 512], F32))
              for i in range(8)]

        ctr = ch(tag + "_tr")
        ctr2 = ch(tag + "_tr2")
        cwl = [ch(tag + "_wl0"), ch(tag + "_wl1")]
        cvl = [ch(tag + "_vl0"), ch(tag + "_vl1")]
        cwa = [ch(tag + "_wla0"), ch(tag + "_wla1")]
        cxl = ch(tag + "_xl")
        cst = ch(tag + "_st")
        ct = ch(tag + "_t")
        cv = ch(tag + "_v")
        ca = ch(tag + "_a")
        W = rec.wait

        def T(fn):
            rec.op("t", fn, tick=ct)

        def V(fn):
            rec.op("v", fn, tick=cv)

        def A(fn):
            rec.op("a", fn, tick=ca)

        W("a", ch("c_wprep"))
        W("g", ch("c_wprep"))
        W("s", ch("c_wprep"))
        crt = ch(tag + "_rt")
        rec.op("a", lambda e: e.dma_start(
            out=rout[:, :].rearrange("p (ch f) -> p ch f", f=E),
            in_=G.routwT_b.rearrange("(ch p) f -> p ch f", p=128)), dma=crt)

        sets = [("e", e, fh) for e in range(E) for fh in range(F // FH)] + \
               [("s", 0, fh) for fh in range(FS // FSH)]

        def load_set(si, buf, part="all"):
            kind, e, fh = sets[si]
            w1v1 = part in ("all", "w1v1")
            w2p = part in ("all", "w2")
            if kind == "e":
                if w1v1:
                    rec.op("g", lambda e_, e=e, fh=fh, buf=buf: e_.dma_start(
                        out=w1s[buf][:, 0:KC * FH].rearrange(
                            "p (ch f) -> p ch f", f=FH),
                        in_=G.w1_b[e].rearrange("(ch p) f -> p ch f", p=128)
                            [:, :, fh * FH:(fh + 1) * FH]), dma=cwl[buf])
                    rec.op("s", lambda e_, e=e, fh=fh, buf=buf: e_.dma_start(
                        out=v1s[buf][:, 0:KC * FH].rearrange(
                            "p (ch f) -> p ch f", f=FH),
                        in_=G.v1_b[e].rearrange("(ch p) f -> p ch f", p=128)
                            [:, :, fh * FH:(fh + 1) * FH]), dma=cvl[buf])
                if w2p:
                    rec.op("g", lambda e_, e=e, fh=fh, buf=buf: e_.dma_start(
                        out=w2s[buf][:, 0:(FH // 128) * D].rearrange(
                            "p (fc h) -> p fc h", h=D),
                        in_=G.w2_b[e, fh * FH:(fh + 1) * FH].rearrange(
                            "(fc p) h -> p fc h", p=128)), dma=cwl[buf])
            else:
                if w1v1:
                    for ech in range(KC):
                        rec.op("a", lambda e_, fh=fh, buf=buf, ech=ech:
                               e_.dma_start(
                            out=w1s[buf][:, ech * FSH:(ech + 1) * FSH],
                            in_=G.sg_b[fh * FSH:(fh + 1) * FSH,
                                       ech * 128:(ech + 1) * 128],
                            transpose=True), dma=cwa[buf])
                        rec.op("a", lambda e_, fh=fh, buf=buf, ech=ech:
                               e_.dma_start(
                            out=v1s[buf][:, ech * FSH:(ech + 1) * FSH],
                            in_=G.su_b[fh * FSH:(fh + 1) * FSH,
                                       ech * 128:(ech + 1) * 128],
                            transpose=True), dma=cwa[buf])
                if w2p:
                    for fc in range(FSH // 128):
                        rec.op("a", lambda e_, fh=fh, buf=buf, fc=fc:
                               e_.dma_start(
                            out=w2s[buf][:, fc * D:(fc + 1) * D],
                            in_=G.sd_b[0:D, fh * FSH + fc * 128:
                                       fh * FSH + (fc + 1) * 128],
                            transpose=True), dma=cwa[buf])

        load_set(0, 0)
        load_set(1, 1)
        W("a", dep)
        W("s", dep)
        wdone = [0, 0]
        xt_free = [0]

        def load_xT(pp_i):
            t0 = pp_i * TP
            rec.wait("a", ct, xt_free[0])
            for ech in range(KC):
                rec.op("a", lambda e, ech=ech, t0=t0: e.dma_start(
                    out=xT[:, ech * TP:(ech + 1) * TP],
                    in_=h_src[t0:t0 + TP, ech * 128:(ech + 1) * 128],
                    transpose=True), dma=ctr)

        load_xT(0)
        for pp_i in range(NP):
            t0 = pp_i * TP
            W("t", ctr)
            W("t", crt)
            for ts in range(NTS):
                W("t", cv)
                for ech in range(KC):
                    T(lambda e, ts=ts, ech=ech: e.matmul(
                        pp[2][:, 0:E],
                        xT[:, ech * TP + ts * 128: ech * TP + (ts + 1) * 128],
                        rout[:, ech * E:(ech + 1) * E],
                        start=(ech == 0), stop=(ech == KC - 1)))
                W("v", ct)
                V(lambda e: e.tensor_copy(sc[:, :], pp[2][:, 0:E]))
                if tag == "f1" and "scores" in G.debug:
                    W("s", cv)
                    rec.op("s", lambda e, ts=ts, t0=t0: e.dma_start(
                        out=G.dbg_scores[t0 + ts * 128:t0 + (ts + 1) * 128, :],
                        in_=sc[:, :]), dma=ch("c_dbgs"))
                    rec.wait("v", ch("c_dbgs"))
                W("v", cv)
                V(lambda e: e.reduce_max(gcol[:, 0:1], sc[:, :], AX.X))
                for r in range(3):
                    cur = sc if r == 0 else sc2
                    mc = gcol[:, 0:1] if r == 0 else gcol[:, 3:4]
                    W("v", cv)
                    V(lambda e, cur=cur, mc=mc: e.tensor_scalar(
                        sc3[:, :], cur[:, :], mc, None, ALU.is_equal))
                    W("v", cv)
                    V(lambda e, cur=cur: e.scalar_tensor_tensor(
                        sc2[:, :], sc3[:, :], NEG, cur[:, :],
                        ALU.mult, ALU.add))
                    W("v", cv)
                    V(lambda e: e.reduce_max(gcol[:, 3:4], sc2[:, :], AX.X))
                W("v", cv)
                V(lambda e: e.tensor_scalar(gcol[:, 5:6], gcol[:, 0:1], -1.0,
                                            None, ALU.mult))
                W("a", cv)
                A(lambda e: e.activation(sc[:, :], sc[:, :], AF.Exp,
                                         bias=gcol[:, 5:6]))
                W("a", ca)
                A(lambda e: e.activation(gcol[:, 3:4], gcol[:, 3:4], AF.Exp,
                                         bias=gcol[:, 5:6]))
                W("v", ca)
                V(lambda e: e.tensor_scalar(sc3[:, :], sc[:, :],
                                            gcol[:, 3:4], None, ALU.is_ge))
                W("v", cv)
                V(lambda e: e.tensor_mul(sc[:, :], sc[:, :], sc3[:, :]))
                W("v", cv)
                V(lambda e: e.reduce_sum(gcol[:, 6:7], sc[:, :], AX.X))
                W("v", cv)
                V(lambda e: e.reciprocal(gcol[:, 7:8], gcol[:, 6:7]))
                W("v", cv)
                V(lambda e, ts=ts: e.tensor_scalar_mul(
                    gates[:, ts * E:(ts + 1) * E], sc[:, :], gcol[:, 7:8]))
                if tag == "f1" and "gates" in G.debug:
                    W("s", cv)
                    rec.op("s", lambda e, ts=ts, t0=t0: e.dma_start(
                        out=G.dbg_gates[t0 + ts * 128:t0 + (ts + 1) * 128, :],
                        in_=gates[:, ts * E:(ts + 1) * E]), dma=ch("c_dbgg"))
            W("v", cv)
            V(lambda e: e.memset(acc[:, :], 0.0))
            heb = [(0, 1), (4, 5)]
            yb = [2, 3, 6, 7]
            he_done = {0: (0, 0), 1: (0, 0)}
            y_done = {k: 0 for k in yb}
            NH = D // 512
            assert NH <= 4
            for si in range(len(sets)):
                buf = si % 2
                kind, e_i, fh = sets[si]
                FL = FH if kind == "e" else FSH
                NF = FL // 128
                W("t", cwl[buf])
                W("t", cvl[buf])
                W("t", cwa[buf])
                for fc in range(NF):
                    par2 = fc % 2
                    pA, pB = pp[heb[par2][0]], pp[heb[par2][1]]
                    rec.wait("t", ca, he_done[par2][0])
                    rec.wait("t", cv, he_done[par2][1])
                    for nh in range(max(1, TP // 512)):
                        nn = min(512, TP)
                        for ech in range(KC):
                            T(lambda e, fc=fc, nh=nh, ech=ech, buf=buf,
                              FL=FL, nn=nn, pA=pA: e.matmul(
                                pA[:, 0:nn],
                                w1s[buf][:, ech * FL + fc * 128:
                                         ech * FL + (fc + 1) * 128],
                                xT[:, ech * TP + nh * 512:
                                   ech * TP + nh * 512 + nn],
                                start=(ech == 0), stop=(ech == KC - 1)))
                        for ech in range(KC):
                            T(lambda e, fc=fc, nh=nh, ech=ech, buf=buf,
                              FL=FL, nn=nn, pB=pB: e.matmul(
                                pB[:, 0:nn],
                                v1s[buf][:, ech * FL + fc * 128:
                                         ech * FL + (fc + 1) * 128],
                                xT[:, ech * TP + nh * 512:
                                   ech * TP + nh * 512 + nn],
                                start=(ech == 0), stop=(ech == KC - 1)))
                        W("a", ct)
                        W("a", cv)
                        A(lambda e, nn=nn, pA=pA, par2=par2: e.activation(
                            st[:, par2 * 512:par2 * 512 + nn],
                            pA[:, 0:nn], AF.Silu))
                        W("v", ca)
                        V(lambda e, fc=fc, nh=nh, nn=nn, pB=pB, par2=par2:
                          e.tensor_mul(
                            heT[:, fc * TP + nh * 512:
                                fc * TP + nh * 512 + nn],
                            st[:, par2 * 512:par2 * 512 + nn],
                            pB[:, 0:nn]))
                    he_done[par2] = (ca.n, cv.n)
                xt_free[0] = ct.n
                nsi = si + 2
                do_pf = nsi < len(sets) or pp_i + 1 < NP
                if nsi >= len(sets):
                    nsi -= len(sets)
                if do_pf:
                    he_w_done = ct.n
                    rec.wait("g", ct, he_w_done)
                    rec.wait("a", ct, he_w_done)
                    rec.wait("s", ct, he_w_done)
                    load_set(nsi, buf, part="w1v1")
                he_ready = cv.n
                for tt in range(NTS):
                    rec.wait("t", cv, he_ready)
                    for hc in range(NH):
                        rec.wait("t", cv, y_done[yb[hc]])
                    for fc in range(NF):
                        for hc in range(NH):
                            T(lambda e, tt=tt, hc=hc, fc=fc, buf=buf, NF=NF,
                              py=pp[yb[hc]]: e.matmul(
                                py[:, 0:512],
                                heT[:, fc * TP + tt * 128:
                                    fc * TP + (tt + 1) * 128],
                                w2s[buf][:, fc * D + hc * 512:
                                         fc * D + (hc + 1) * 512],
                                start=(fc == 0), stop=(fc == NF - 1)))
                    ymm_done = ct.n
                    for hc in range(NH):
                        rec.wait("v", ct, ymm_done)
                        rec.wait("v", cv, y_done[yb[hc]])
                        if kind == "e":
                            V(lambda e, tt=tt, hc=hc, e_i=e_i,
                              py=pp[yb[hc]]: e.scalar_tensor_tensor(
                                acc[:, tt * D + hc * 512:
                                    tt * D + (hc + 1) * 512],
                                py[:, 0:512],
                                gates[:, tt * E + e_i: tt * E + e_i + 1],
                                acc[:, tt * D + hc * 512:
                                    tt * D + (hc + 1) * 512],
                                ALU.mult, ALU.add))
                        else:
                            V(lambda e, tt=tt, hc=hc, py=pp[yb[hc]]:
                              e.tensor_add(
                                acc[:, tt * D + hc * 512:
                                    tt * D + (hc + 1) * 512],
                                acc[:, tt * D + hc * 512:
                                    tt * D + (hc + 1) * 512],
                                py[:, 0:512]))
                        y_done[yb[hc]] = cv.n
                wdone[buf] = cv.n
                if do_pf:
                    y_w_done = ct.n
                    rec.wait("g", ct, y_w_done)
                    rec.wait("a", ct, y_w_done)
                    load_set(nsi, buf, part="w2")
            if pp_i + 1 < NP:
                load_xT(pp_i + 1)
            b_eps = G.cst(1e-5)
            for tt in range(NTS):
                xo = (xa, xb)[tt % 2]
                W("s", ca)
                W("s", cv)
                rec.op("s", lambda e, tt=tt, t0=t0, xo=xo: e.dma_start(
                    out=xo[:, :], in_=x_src[t0 + tt * 128:t0 + (tt + 1) * 128, :]),
                    dma=cxl)
                W("v", cxl)
                W("v", ca)
                W("v", cst)
                V(lambda e, tt=tt, xo=xo: e.tensor_add(
                    xt[:, :], acc[:, tt * D:(tt + 1) * D], xo[:, :]))
                if last:
                    W("s", cv)
                    rec.op("s", lambda e, tt=tt, t0=t0: e.dma_start(
                        out=x_dst[t0 + tt * 128:t0 + (tt + 1) * 128, :],
                        in_=xt[:, :]), dma=cst)
                else:
                    W("v", cv)
                    V(lambda e: e.reduce_sum(cols[:, 0:1], xt[:, :], AX.X))
                    W("a", cv)
                    A(lambda e: e.activation(cols[:, 1:2], cols[:, 0:1],
                                             AF.Copy, scale=1.0 / D))
                    W("v", ca)
                    V(lambda e: e.tensor_scalar_sub(xt[:, :], xt[:, :],
                                                    cols[:, 1:2]))
                    W("a", cv)
                    A(lambda e: e.activation(sq[:, :], xt[:, :], AF.Square))
                    W("v", ca)
                    V(lambda e: e.reduce_sum(cols[:, 2:3], sq[:, :], AX.X))
                    W("a", cv)
                    A(lambda e, b=b_eps: e.activation(
                        cols[:, 3:4], cols[:, 2:3], AF.Sqrt, scale=1.0 / D,
                        bias=b))
                    W("v", ca)
                    V(lambda e: e.reciprocal(cols[:, 2:3], cols[:, 3:4]))
                    W("v", cv)
                    V(lambda e: e.tensor_scalar_mul(hb[:, :], xt[:, :],
                                                    cols[:, 2:3]))
                    W("v", cv)
                    V(lambda e: e.tensor_scalar_add(xt[:, :], xt[:, :],
                                                    cols[:, 1:2]))
                    W("s", cv)
                    rec.op("s", lambda e, tt=tt, t0=t0: e.dma_start(
                        out=x_dst[t0 + tt * 128:t0 + (tt + 1) * 128, :],
                        in_=xt[:, :]), dma=cst)
                    rec.op("s", lambda e, tt=tt, t0=t0: e.dma_start(
                        out=h_dst[t0 + tt * 128:t0 + (tt + 1) * 128, :],
                        in_=hb[:, :]), dma=cst)
        W("s", cst)
        W("g", cst)


def p_attn_b(G):
    c, nc, rec, ch = G.c, G.nc, G.rec, G.ch
    HPC, KC, S_, D = c.HPC, c.KC, c.S, c.D
    TQ = c.TQ
    NT = S_ // TQ
    NKT = S_ // 128
    with ExitStack() as S:
        def sb(n, shape, dt):
            return S.enter_context(nc.sbuf_tensor("ab_" + n, list(shape), dt))

        hT = [sb("hT0", (128, KC * TQ), BF16), sb("hT1", (128, KC * TQ), BF16)]
        wqT = sb("wqT", (128, KC * HPC * 128), BF16)
        wkT = sb("wkT", (128, KC * HPC * 128), BF16)
        wvT = sb("wvT", (128, KC * HPC * 128), BF16)
        qT = sb("qT", (128, HPC * S_), BF16)
        kT = sb("kT", (128, HPC * S_), BF16)
        vb = sb("vb", (128, HPC * NKT * 132), BF16)
        cosb = sb("cosb", (128, S_), F32)
        sinb = sb("sinb", (128, S_), F32)
        rp = sb("rp", (128, 128), BF16)
        bqk = sb("bqk", (128, 2 * HPC), F32)
        bvb = sb("bvb", (128, HPC * 128), F32)
        brow = sb("brow", (1, 3 * HPC * 128), F32)
        dmask = sb("dmask", (128, 4 * 512), F32)
        rt1 = sb("rt1", (128, 512), F32)
        rt1b = sb("rt1b", (128, 512), BF16)
        rt2 = sb("rt2", (128, 512), F32)
        rt3 = sb("rt3", (128, 512), F32)
        Pb = [sb("Pb0", (128, 512), BF16), sb("Pb1", (128, 512), BF16)]
        onb = sb("onb", (128, 4 * HPC * 128), BF16)
        ocol = sb("ocol", (128, 8), F32)
        pp = [S.enter_context(nc.psum_tensor(f"ab_p{i}", [128, 512], F32))
              for i in range(6)]
        po = [pp[0], pp[1], pp[2], pp[5]]

        cld = ch("cb_ld")
        cwl = ch("cb_wl")
        ct = ch("cb_t")
        cv = ch("cb_v")
        ca = ch("cb_a")
        cg = ch("cb_g")
        cso = ch("cb_so")
        W = rec.wait

        def T(fn):
            rec.op("t", fn, tick=ct)

        def V(fn):
            rec.op("v", fn, tick=cv)

        def A(fn):
            rec.op("a", fn, tick=ca)

        W("a", ch("c_wprep"))
        for ech in range(KC):
            for wsrc, wdst in ((G.wq_b, wqT), (G.wk_b, wkT), (G.wv_b, wvT)):
                rec.op("a", lambda e, ech=ech, ws=wsrc, wd=wdst: e.dma_start(
                    out=wd[:, ech * HPC * 128:(ech + 1) * HPC * 128],
                    in_=ws[0:HPC * 128, ech * 128:(ech + 1) * 128],
                    transpose=True), dma=cwl)
        cws = ch("cb_ws")
        rec.op("s", lambda e: e.dma_start(out=cosb[:, :], in_=G.cosb), dma=cws)
        rec.op("s", lambda e: e.dma_start(out=sinb[:, :], in_=G.sinb), dma=cws)
        rec.op("g", lambda e: e.dma_start(out=rp[:, :], in_=G.ropeperm),
               dma=ch("cb_wg"))
        for r in range(3):
            rec.op("s", lambda e, r=r: e.dma_start(
                out=brow[0:1, r * HPC * 128:(r + 1) * HPC * 128],
                in_=G.bqkv[r:r + 1, :]), dma=cws)
        W("t", cwl)
        W("t", ch("cb_ws"))
        W("t", ch("cb_wg"))
        for qk in range(2):
            for hh in range(HPC):
                T(lambda e, qk=qk, hh=hh: e.matmul(
                    pp[0][:, qk * HPC + hh: qk * HPC + hh + 1],
                    brow[0:1, (qk * HPC + hh) * 128:(qk * HPC + hh + 1) * 128],
                    G.s_one11_f[:, :], start=True, stop=True))
        W("v", ct)
        V(lambda e: e.tensor_copy(bqk[:, :], pp[0][:, 0:2 * HPC]))
        for f in range(0, HPC * 128, 512):
            nn = min(512, HPC * 128 - f)
            W("t", cv)
            T(lambda e, f=f, nn=nn: e.matmul(
                pp[0][:, 0:nn], G.s_ones_row_f[:1, :128],
                brow[0:1, 2 * HPC * 128 + f: 2 * HPC * 128 + f + nn],
                start=True, stop=True))
            W("v", ct)
            V(lambda e, f=f, nn=nn: e.tensor_copy(bvb[:, f:f + nn],
                                                  pp[0][:, 0:nn]))
        for m in range(4):
            rec.op("g", lambda e, m=m: e.memset(
                dmask[:, m * 512:(m + 1) * 512], 0.0), tick=cg)
        W("g", cg)
        for m in range(4):
            rec.op("g", lambda e, m=m: e.affine_select(
                dmask[:, m * 512:(m + 1) * 512],
                dmask[:, m * 512:(m + 1) * 512],
                [[1, 512]], ALU.is_ge, NEG, base=-(m * 128),
                channel_multiplier=-1), tick=cg)

        W("s", ch("cc_h3"))
        b_sh = G.cst(-8.0)

        def load_hT(b, i, par):
            for chk in range(KC):
                rec.op("s", lambda e, b=b, i=i, par=par, chk=chk: e.dma_start(
                    out=hT[par][:, chk * TQ:(chk + 1) * TQ],
                    in_=G.ag_h3[b * S_ + i * TQ: b * S_ + (i + 1) * TQ,
                                chk * 128:(chk + 1) * 128],
                    transpose=True), dma=cld)

        for b in range(c.B):
            if b == 0:
                load_hT(0, 0, 0)
            W("v", ct)
            W("v", ca)
            V(lambda e: e.memset(vb[:, :], 0.0))
            W("v", cv)
            for hh in range(HPC):
                for kt in range(NKT):
                    V(lambda e, hh=hh, kt=kt: e.memset(
                        vb[:, (hh * NKT + kt) * 132 + 128:
                           (hh * NKT + kt) * 132 + 129], 1.0))
            for i in range(NT):
                par = i % 2
                W("t", cld)
                W("t", cg)
                for hh in range(HPC):
                    for qk, (wT, dstT) in enumerate(((wqT, qT), (wkT, kT))):
                        for nh in range(TQ // 512):
                            tpos = i * TQ + nh * 512
                            W("t", cv)
                            W("t", ca)
                            for ech in range(KC):
                                T(lambda e, hh=hh, wT=wT, nh=nh, ech=ech,
                                  par=par: e.matmul(
                                    pp[0][:, 0:512],
                                    wT[:, ech * HPC * 128 + hh * 128:
                                       ech * HPC * 128 + (hh + 1) * 128],
                                    hT[par][:, ech * TQ + nh * 512:
                                            ech * TQ + (nh + 1) * 512],
                                    start=(ech == 0), stop=(ech == KC - 1)))
                            W("v", ct)
                            V(lambda e, hh=hh, qk=qk: e.tensor_scalar_add(
                                rt1[:, :], pp[0][:, 0:512],
                                bqk[:, qk * HPC + hh: qk * HPC + hh + 1]))
                            W("a", cv)
                            W("a", ct)
                            A(lambda e: e.activation(rt1b[:, :], rt1[:, :],
                                                     AF.Copy))
                            W("t", ca)
                            T(lambda e: e.matmul(pp[1][:, 0:512], rp[:, :],
                                                 rt1b[:, :],
                                                 start=True, stop=True))
                            W("v", ct)
                            rec.wait("v", ch("cb_ws"))
                            V(lambda e, tpos=tpos: e.tensor_mul(
                                rt2[:, :], rt1[:, :],
                                cosb[:, tpos:tpos + 512]))
                            W("v", cv)
                            V(lambda e, tpos=tpos: e.tensor_mul(
                                rt3[:, :], pp[1][:, 0:512],
                                sinb[:, tpos:tpos + 512]))
                            W("v", cv)
                            V(lambda e, hh=hh, dstT=dstT, tpos=tpos:
                              e.tensor_add(
                                dstT[:, hh * S_ + tpos: hh * S_ + tpos + 512],
                                rt2[:, :], rt3[:, :]))
                for ts in range(TQ // 128):
                    W("t", cv)
                    W("t", ca)
                    for ech in range(KC):
                        T(lambda e, ts=ts, ech=ech, par=par: e.matmul(
                            pp[2][:, 0:HPC * 128],
                            hT[par][:, ech * TQ + ts * 128:
                                    ech * TQ + (ts + 1) * 128],
                            wvT[:, ech * HPC * 128:(ech + 1) * HPC * 128],
                            start=(ech == 0), stop=(ech == KC - 1)))
                    kt = i * (TQ // 128) + ts
                    W("v", ct)
                    for hh in range(HPC):
                        V(lambda e, hh=hh, kt=kt: e.tensor_add(
                            vb[:, (hh * NKT + kt) * 132:
                               (hh * NKT + kt) * 132 + 128],
                            pp[2][:, hh * 128:(hh + 1) * 128],
                            bvb[:, hh * 128:(hh + 1) * 128]))
                if i + 1 < NT:
                    rec.wait("s", ct)
                    load_hT(b, i + 1, (i + 1) % 2)
                elif b + 1 < c.B:
                    rec.wait("s", ct)
                    load_hT(b + 1, 0, 0)
            for qt in range(S_ // 512):
                for hh in range(HPC):
                    nk = 4 * qt + 4
                    W("t", cv)
                    for kt in range(nk):
                        pbuf = kt % 2
                        ps = pp[3 + pbuf]
                        W("t", ca)
                        T(lambda e, hh=hh, qt=qt, kt=kt, ps=ps: e.matmul(
                            ps[:, 0:512],
                            kT[:, hh * S_ + kt * 128: hh * S_ + (kt + 1) * 128],
                            qT[:, hh * S_ + qt * 512: hh * S_ + (qt + 1) * 512],
                            start=True, stop=True))
                        diag = kt - 4 * qt
                        if diag >= 0:
                            W("v", ct)
                            V(lambda e, ps=ps, diag=diag: e.tensor_add(
                                ps[:, 0:512], ps[:, 0:512],
                                dmask[:, diag * 512:(diag + 1) * 512]))
                            W("a", cv)
                        else:
                            W("a", ct)
                        W("a", cv)
                        W("a", ch("cb_pv"))
                        A(lambda e, ps=ps, pbuf=pbuf, b=b_sh: e.activation(
                            Pb[pbuf][:, :], ps[:, 0:512], AF.Exp,
                            scale=float(128 ** -0.5), bias=b))
                        W("t", ca)
                        for sq in range(4):
                            rec.op("t", lambda e, pbuf=pbuf, sq=sq, hh=hh,
                                   kt=kt, nk=nk: e.matmul(
                                po[sq][:, 0:132],
                                Pb[pbuf][:, sq * 128:(sq + 1) * 128],
                                vb[:, (hh * NKT + kt) * 132:
                                   (hh * NKT + kt + 1) * 132],
                                start=(kt == 0), stop=(kt == nk - 1)),
                                tick=ch("cb_pv"))
                    W("v", ch("cb_pv"))
                    W("v", cso)
                    for sq in range(4):
                        V(lambda e, sq=sq: e.reciprocal(
                            ocol[:, sq:sq + 1], po[sq][:, 128:129]))
                        W("v", cv)
                        V(lambda e, sq=sq, hh=hh: e.tensor_scalar_mul(
                            onb[:, (sq * HPC + hh) * 128:
                                (sq * HPC + hh + 1) * 128],
                            po[sq][:, 0:128], ocol[:, sq:sq + 1]))
                        W("v", cv)
                W("s", cv)
                for sq in range(4):
                    r0 = b * S_ + qt * 512 + sq * 128
                    rec.op("s", lambda e, sq=sq, r0=r0: e.dma_start(
                        out=G.agin_nb[r0:r0 + 128, :],
                        in_=onb[:, sq * HPC * 128:(sq + 1) * HPC * 128]),
                        dma=cso)
        W("g", cso)
    rec.op("g", lambda e: e.collective_compute(
        "AllGather", ALU.bypass, replica_groups=[list(range(c.NC))],
        ins=[G.agin_nb[:, :]], outs=[G.ag_nb[:, :]]), tick=ch("cc_nb"))
    G.dbg("nb", G.ag_nb, after_ch=ch("cc_nb"))





def host_consts(c):
    i128 = np.arange(128)
    pos = np.arange(c.S, dtype=np.float64)[:, None]
    theta = np.repeat(-np.arange(0, 128, 2, dtype=np.float64) / 128.0, 2)
    theta = np.power(10000.0, theta)
    ang = pos * theta[None, :]
    perm = np.zeros((128, 128), dtype=np.float32)
    for i in range(0, 128, 2):
        perm[i, i + 1] = -1.0
        perm[i + 1, i] = 1.0
    return {
        "identf": np.eye(128, dtype=np.float32),
        "triu1": np.triu(np.ones((128, 128), dtype=np.float32)),
        "e127": np.broadcast_to((i128[:, None] == 127), (128, 128))
                .astype(np.float32).copy(),
        "cosb": np.ascontiguousarray(np.cos(ang).T.astype(np.float32)),
        "sinb": np.ascontiguousarray(np.sin(ang).T.astype(np.float32)),
        "ropeperm": perm.T.copy(),
    }


def shard_inputs(c, inp, core):
    hp = c.HPC
    h0 = core * hp
    f32 = lambda k: np.asarray(inp[k], dtype=np.float32)
    x = f32("L0_attn_a_hidden_states").reshape(c.T, c.D)
    conv = f32("L0_attn_a_conv_weight")
    convrows = []
    for hh in range(hp):
        h = h0 + hh
        for part in range(3):
            convrows.append(conv[part * c.D + 128 * h: part * c.D + 128 * (h + 1)])
    C = np.ascontiguousarray
    m = {
        "x0": C(x[core * c.Tc:(core + 1) * c.Tc]),
        "wqkvz": C(f32("L0_attn_a_W_qkvz")[:, 512 * h0: 512 * (h0 + hp)]),
        "wba": C(f32("L0_attn_a_W_ba")[:, 2 * h0: 2 * (h0 + hp)]),
        "convw": C(np.concatenate(convrows, axis=0)),
        "dtb_alog": C(np.stack([f32("L0_attn_a_dt_bias")[h0:h0 + hp],
                                f32("L0_attn_a_A_log")[h0:h0 + hp]])),
        "normw": C(f32("L0_attn_a_norm_weight")[None, :]),
        "wout": C(f32("L0_attn_a_W_out")),
        "wq": C(f32("L0_attn_b_Wq")[128 * h0:128 * (h0 + hp)]),
        "wk": C(f32("L0_attn_b_Wk")[128 * h0:128 * (h0 + hp)]),
        "wv": C(f32("L0_attn_b_Wv")[128 * h0:128 * (h0 + hp)]),
        "bqkv": C(np.stack([f32("L0_attn_b_bq")[128 * h0:128 * (h0 + hp)],
                            f32("L0_attn_b_bk")[128 * h0:128 * (h0 + hp)],
                            f32("L0_attn_b_bv")[128 * h0:128 * (h0 + hp)]])),
        "wo": C(f32("L0_attn_b_Wo")),
        "bo": C(f32("L0_attn_b_bo")[None, :]),
        "routwT": C(f32("L0_ffn_router_w").T),
        "w1": C(f32("L0_ffn_w1")),
        "v1": C(f32("L0_ffn_v1")),
        "w2": C(f32("L0_ffn_w2")),
        "sg": C(f32("L0_ffn_sg_w")),
        "su": C(f32("L0_ffn_su_w")),
        "sd": C(f32("L0_ffn_sd_w")),
    }
    m.update(host_consts(c))
    return m


def kernel(**inputs):
    c = CFG()
    G = build_nc(c)
    in_maps = [shard_inputs(c, inputs, core) for core in range(c.NC)]
    res = run_bass_kernel_spmd(G.nc, in_maps, core_ids=list(range(c.NC)))
    outs = [np.asarray(res.results[i]["out"]) for i in range(c.NC)]
    return np.concatenate(outs, axis=0).reshape(c.B, c.S, c.D)
